# version 28
# baseline (speedup 1.0000x reference)
import os
import numpy as np
import ml_dtypes
from contextlib import ExitStack

_DBG = os.environ.get("KDBG", "")
_DPI = int(os.environ.get("KDPI", "0"))

import concourse.bass as bass
import concourse.mybir as mybir
from concourse.bass_utils import run_bass_kernel_spmd

F32 = mybir.dt.float32
BF16 = mybir.dt.bfloat16
AF = mybir.ActivationFunctionType
ALU = mybir.AluOpType
AX = mybir.AxisListType
BFD = ml_dtypes.bfloat16

P = 128
D = 2048
DB = 16
NH = 16
EXP = 8
HBE = 11
HBS = 22
SEQ_PER_CORE = 2
N_CORES = 8


class Buf:
    __slots__ = ("name", "key", "lo", "hi", "w", "readers", "small")

    def __init__(self, name, key, lo, hi, small=False):
        self.name, self.key, self.lo, self.hi = name, key, lo, hi
        self.w = None
        self.readers = {}
        self.small = small


class Sched:

    N_DMA_LANES = 8

    def __init__(self):
        self.prog = {}
        self.count = {}
        self.water = {}
        self.inc_amt = {}
        self.semkeys = {}
        self.rot = {}
        self.sems = {}
        self.registry = {}

    SECTION = {"t": "t", "v": "v", "s": "s", "q": "q", "g": "g", "p": "g"}

    def add_engine(self, name, inc):
        sec = self.SECTION[name]
        if sec not in self.prog:
            self.prog[sec] = []
            self.water[sec] = {}
        self.inc_amt[name] = inc
        if inc == 16:
            self.semkeys[name] = [f"{name}{i}" for i in range(self.N_DMA_LANES)]
        else:
            self.semkeys[name] = [name]
        self.rot[name] = 0
        for k in self.semkeys[name]:
            self.count[k] = 0

    def buf(self, name, key, lo=0, hi=1 << 40, small=False):
        b = Buf(name, key, lo, hi, small)
        self.registry.setdefault(key, []).append(b)
        return b

    def op(self, en, fn, reads=(), writes=(), war_reads=()):
        waits = {}
        sec = self.SECTION[en]
        own = set()
        for e2, s2 in self.SECTION.items():
            if s2 == sec and e2 in self.rot:
                own.update(self.semkeys[e2])

        def need(tok, force_same=False):
            sem_key, val = tok
            if sem_key in own and sec != "g" and not force_same:
                return
            if sec != "g" and self.water[sec].get(sem_key, -1) >= val:
                return
            if waits.get(sem_key, -1) < val:
                waits[sem_key] = val

        eff_writes = []
        for b in writes:
            for o in self.registry.get(b.key, ()):
                if o.lo < b.hi and b.lo < o.hi:
                    eff_writes.append(o)
        for b in reads:
            if b.w:
                need(b.w, force_same=b.small)
        for b in eff_writes:
            if b.w:
                need(b.w, force_same=b.small)
            for tok in b.readers.values():
                need(tok, force_same=b.small)
        for k, v in waits.items():
            self.water[sec][k] = v
        inc = self.inc_amt[en]
        keys = self.semkeys[en]
        semkey = keys[self.rot[en] % len(keys)]
        self.rot[en] += 1
        if inc == 16 and self.count[semkey] > 0:
            if waits.get(semkey, -1) < self.count[semkey]:
                waits[semkey] = self.count[semkey]
        self.count[semkey] += inc
        tok = (semkey, self.count[semkey])
        for b in eff_writes:
            b.w = tok
            b.readers = {}
        for b in reads:
            b.readers[sec] = tok
        for b in war_reads:
            b.readers[sec] = tok
        self.prog[sec].append((sorted(waits.items()), fn, inc, semkey))
        return tok

    def wait_only(self, en, tok):
        self.prog[self.SECTION[en]].append(([tok], None, 0, None))

    def run(self, en, eng):
        for e2, s2 in self.SECTION.items():
            if s2 == en and e2 in self.rot:
                for k in self.semkeys[e2]:
                    eng.sem_clear(self.sems[k])
        for r in range(2):
            eng.sem_clear(self.boot[(en, r)])
        for r in range(2):
            eng.sem_inc(self.boot[(en, r)], 1)
            for other in self.prog:
                if other != en:
                    eng.wait_ge(self.boot[(other, r)], 1)
        for waits, fn, inc, semkey in self.prog[en]:
            for sem_key, val in waits:
                eng.wait_ge(self.sems[sem_key], val)
            if fn is not None:
                ins = fn(eng)
                if inc:
                    ins.then_inc(self.sems[semkey], inc)


def build_program():
    nc = bass.Bass("TRN2", target_bir_lowering=False)

    x_h = nc.dram_tensor("x", [SEQ_PER_CORE, D, D], F32, kind="ExternalInput")
    out_h = nc.dram_tensor("out", [SEQ_PER_CORE, D, D], F32, kind="ExternalOutput")
    wq_h = nc.dram_tensor("wq", [NH, P, DB * P], F32, kind="ExternalInput")
    wo_h = nc.dram_tensor("wo", [4, P, DB * 512], F32, kind="ExternalInput")
    ws_h = nc.dram_tensor("ws", [P, DB * NH], F32, kind="ExternalInput")
    gw_h = nc.dram_tensor("gw", [P, DB * EXP], F32, kind="ExternalInput")
    bs_h = nc.dram_tensor("bs", [NH, 1], F32, kind="ExternalInput")
    lng_h = nc.dram_tensor("lng", [P, 1], F32, kind="ExternalInput")
    lnb_h = nc.dram_tensor("lnb", [P, 1], F32, kind="ExternalInput")
    w012_h = nc.dram_tensor("w012", [P, 3 * NH * P], F32, kind="ExternalInput")
    wg_h = nc.dram_tensor("wg", [EXP, HBE, P, DB * P], F32, kind="ExternalInput")
    wu_h = nc.dram_tensor("wu", [EXP, HBE, P, DB * P], F32, kind="ExternalInput")
    wd_h = nc.dram_tensor("wd", [EXP, 4, P, HBE * 512], F32, kind="ExternalInput")
    swg_h = nc.dram_tensor("swg", [HBS, P, DB * P], F32, kind="ExternalInput")
    swu_h = nc.dram_tensor("swu", [HBS, P, DB * P], F32, kind="ExternalInput")
    swd_h = nc.dram_tensor("swd", [8, P, HBS * 256], F32, kind="ExternalInput")
    ident_h = nc.dram_tensor("ident", [P, P], BF16, kind="ExternalInput")
    if "dump" in _DBG:
        dxnT_h = nc.dram_tensor("dxnT", [P, DB * 1024], BF16, kind="ExternalOutput")
        dxgT_h = nc.dram_tensor("dxgT", [P, DB * 1024], BF16, kind="ExternalOutput")
        dcomb_h = nc.dram_tensor("dcomb", [P, 8 * EXP], F32, kind="ExternalOutput")
        dsT_h = nc.dram_tensor("dsT", [NH, 1024], BF16, kind="ExternalOutput")
        dqp_h = nc.dram_tensor("dqp", [P, DB * 1024], BF16, kind="ExternalOutput")
        dom_h = nc.dram_tensor("dom", [P, DB * 1024], BF16, kind="ExternalOutput")
        dg_h = nc.dram_tensor("dg", [P, HBE * 1024], BF16, kind="ExternalOutput")
        dgt_h = nc.dram_tensor("dgt", [P, 192], F32, kind="ExternalOutput")
        dst_h = nc.dram_tensor("dstat", [P, 48], F32, kind="ExternalOutput")

    sched = Sched()
    for en, inc in (("t", 1), ("v", 1), ("s", 1), ("q", 16), ("g", 16), ("p", 1)):
        sched.add_engine(en, inc)

    with ExitStack() as ctx:
        e = ctx.enter_context

        ident = e(nc.sbuf_tensor("identb", [P, P], BF16)).ap()
        dgt_sb = e(nc.sbuf_tensor("dgt_sb", [P, 8 * 24], F32)).ap() if "dump" in _DBG else None
        B_dgt = sched.buf("dgt", "dgt") if "dump" in _DBG else None
        ones_col = e(nc.sbuf_tensor("ones_col", [P, 1], BF16)).ap()
        ones_row = e(nc.sbuf_tensor("ones_row", [1, P], BF16)).ap()
        ones_blk = e(nc.sbuf_tensor("ones_blk", [P, P], BF16)).ap()
        w012 = e(nc.sbuf_tensor("w012b", [P, 3 * NH * P], BF16)).ap()
        ws_sb = e(nc.sbuf_tensor("ws_sb", [P, DB * NH], BF16)).ap()
        gw_sb = e(nc.sbuf_tensor("gw_sb", [P, DB * EXP], BF16)).ap()
        bs_sb = e(nc.sbuf_tensor("bs_sb", [NH, 1], F32)).ap()
        lng = e(nc.sbuf_tensor("lng_sb", [P, 1], F32)).ap()
        lnb = e(nc.sbuf_tensor("lnb_sb", [P, 1], F32)).ap()
        csb = e(nc.sbuf_tensor("csb", [P, 8], F32)).ap()
        acc = e(nc.sbuf_tensor("acc", [P, 8 * D], F32)).ap()
        xnT = e(nc.sbuf_tensor("xnT", [P, DB * 1024], BF16)).ap()
        xgom = e(nc.sbuf_tensor("xgom", [P, DB * 1024], BF16)).ap()
        gq = e(nc.sbuf_tensor("gqb", [P, 2 * HBE * 1024], BF16)).ap()
        upb = [e(nc.sbuf_tensor(f"upb{i}", [P, DB * P], BF16)).ap() for i in range(3)]
        sT = e(nc.sbuf_tensor("sT_sb", [NH, 1024], BF16)).ap()
        comb = e(nc.sbuf_tensor("comb_sb", [P, 8 * EXP], F32)).ap()
        stat = e(nc.sbuf_tensor("stat_sb", [P, 48], F32)).ap()
        gst = e(nc.sbuf_tensor("gst_sb", [P, 192], F32)).ap()
        psA = e(nc.psum_tensor("psA", [P, 1024], F32)).ap()
        psB = e(nc.psum_tensor("psB", [P, 1024], F32)).ap()
        psC = e(nc.psum_tensor("psC", [P, 1024], F32)).ap()
        psD = e(nc.psum_tensor("psD", [P, 512], F32)).ap()
        psT = e(nc.psum_tensor("psTb", [P, 1024], BF16)).ap()

        sched.boot = {}
        for en in ("t", "v", "s", "q", "g", "p"):
            for k in sched.semkeys[en]:
                sched.sems[k] = e(nc.semaphore(f"sem_{k}"))
        for en in ("t", "v", "s", "q", "g"):
            for r in range(2):
                sched.boot[(en, r)] = e(nc.semaphore(f"boot_{en}{r}"))

        gq_f32 = gq.bitcast(F32)
        stage_f32 = [gq_f32[:, 0:2048], gq_f32[:, 2048:4096]]
        xnstage = [gq[:, 8192:10240], gq[:, 10240:12288]]
        sqscr2l = [gq_f32[:, 6144:8192], gq_f32[:, 8192:10240]]
        qn_all = gq[:, 0:16384]
        wo_chunk = [gq[:, 0:8192], gq[:, 8192:16384]]
        g_buf = [gq[:, 0:11264], gq[:, 11264:22528]]
        gs_full = gq[:, 0:HBS * 1024]
        wd_chunk = [xgom[:, 0:5632], xgom[:, 5632:11264]]
        sil_f32 = [xgom.bitcast(F32)[:, 5632:6144], xgom.bitcast(F32)[:, 6144:6656]]
        qsq = upb[0][:, 0:1024]
        hid = upb[0][:, 1024:2048]
        O_sb = upb[1].bitcast(F32)[:, 0:1024]
        msq_row = upb[2].bitcast(F32)[0:1, 0:1024]
        r_row = gq_f32[0:1, 8192:9216]
        m_row = gq_f32[0:1, 9216:10240]
        brow = gq[0:1, 20480:21504]
        srow_bf = gq[0:1, 21504:22528]

        B_const = sched.buf("const", "const")
        B_acc = [sched.buf(f"acc{i}", "acc", i * D, (i + 1) * D) for i in range(8)]
        B_xnT = sched.buf("xnT", "xnT")
        B_xg = sched.buf("xg", "xgom", 0, 16384)
        B_xnt = [sched.buf(f"xnt{i}", "xnt", i, i + 1) for i in range(8)]
        B_xgt = [sched.buf(f"xgt{i}", "xgt", i, i + 1) for i in range(8)]
        B_sl = [sched.buf(f"sl{k}", "sslot", k, k + 1) for k in range(6)]
        B_bfq = [sched.buf(f"bfq{k}", "browf", k, k + 1) for k in range(NH)]
        B_q2p = [sched.buf(f"q2p{k}", "q2par", k, k + 1) for k in range(2)]
        B_om = [sched.buf(f"om{i}", "xgom", i * 1024, (i + 1) * 1024) for i in range(NH)]
        B_dch = [sched.buf("dch0", "xgom", 0, 5632), sched.buf("dch1", "xgom", 5632, 11264)]
        B_sil = [sched.buf("sil0", "xgom", 11264, 12288), sched.buf("sil1", "xgom", 12288, 13312)]
        B_stage = [sched.buf("stage0", "gq", 0, 4096), sched.buf("stage1", "gq", 4096, 8192)]
        B_xnst = [sched.buf("xnst0", "gq", 8192, 10240), sched.buf("xnst1", "gq", 10240, 12288)]
        B_sqscr = [sched.buf("sqscr0", "gq", 12288, 16384),
                   sched.buf("sqscr1", "gq", 16384, 20480)]
        B_qn = [sched.buf(f"qn{i}", "gq", i * 1024, (i + 1) * 1024) for i in range(NH)]
        B_wo = [sched.buf("wo0", "gq", 0, 8192), sched.buf("wo1", "gq", 8192, 16384)]
        B_g = [sched.buf("g0", "gq", 0, 11264), sched.buf("g1", "gq", 11264, 22528)]
        B_up = [sched.buf(f"up{i}", f"upb{i}") for i in range(3)]
        B_scr0 = sched.buf("scr0", "upb0")
        B_scr1 = sched.buf("scr1", "upb1")
        B_scr2 = sched.buf("scr2", "upb2")
        B_sT = sched.buf("sT", "sT")
        B_comb = sched.buf("comb", "comb", small=True)
        B_stat = [sched.buf("statA", "stat", 0, 24, small=True), sched.buf("statB", "stat", 24, 48, small=True)]
        B_lnst = [sched.buf(f"lnst{i}", "stat", i * 12, (i + 1) * 12, small=True) for i in range(4)]
        B_lnact = [sched.buf(f"lnact{i}", "stat", i * 12 + 4, i * 12 + 6, small=True) for i in range(4)]
        B_gst = sched.buf("gst", "gst", small=True)
        B_rowR = sched.buf("rowR", "gq", 16384, 18432, small=True)
        B_rowM = sched.buf("rowM", "gq", 18432, 20480, small=True)
        B_rowB = sched.buf("rowB", "gq", 20480, 21504, small=True)
        B_rowS = sched.buf("rowS", "gq", 21504, 22528, small=True)
        B_psA0 = sched.buf("psA0", "psA", 0, 512)
        B_psA1 = sched.buf("psA1", "psA", 512, 1024)
        B_psB0 = sched.buf("psB0", "psB", 0, 512)
        B_psB1 = sched.buf("psB1", "psB", 512, 1024)
        B_psC0 = sched.buf("psC0", "psC", 0, 512)
        B_psC1 = sched.buf("psC1", "psC", 512, 1024)
        B_psD = sched.buf("psD", "psD")
        B_psT = sched.buf("psT", "psT")
        ps_halves = [(psA[:, 0:512], B_psA0), (psA[:, 512:1024], B_psA1),
                     (psB[:, 0:512], B_psB0), (psB[:, 512:1024], B_psB1)]
        ps4 = [(psD, B_psD)] + ps_halves[:3]

        sched.op("q", lambda q: q.dma_start(out=ident, in_=ident_h.ap()), writes=[B_const])
        sched.op("q", lambda q: q.dma_start(out=bs_sb, in_=bs_h.ap()), writes=[B_const])
        sched.op("q", lambda q: q.dma_start(out=lng, in_=lng_h.ap()), writes=[B_const])
        sched.op("q", lambda q: q.dma_start(out=lnb, in_=lnb_h.ap()), writes=[B_const])
        sched.op("g", lambda g: g.dma_start(out=w012, in_=w012_h.ap()), writes=[B_const])
        sched.op("g", lambda g: g.dma_start(out=ws_sb, in_=ws_h.ap()), writes=[B_const])
        sched.op("g", lambda g: g.dma_start(out=gw_sb, in_=gw_h.ap()), writes=[B_const])
        sched.op("v", lambda v: v.memset(ones_col, 1.0), writes=[B_const])
        sched.op("v", lambda v: v.memset(ones_row, 1.0), writes=[B_const])
        sched.op("v", lambda v: v.memset(ones_blk, 1.0), writes=[B_const])
        c_eps5, c_eps6, c_neg1, c_negh, c_half, c_i2048 = (csb[:, i:i + 1] for i in range(6))
        for _ap, _val in ((c_eps5, 1e-5), (c_eps6, 1e-6), (c_neg1, -1.0),
                          (c_negh, -0.5), (c_half, 0.5), (c_i2048, 1.0 / 2048.0)):
            sched.op("v", lambda v, _ap=_ap, _val=_val: v.memset(_ap, _val),
                     writes=[B_const])

        def ln_tile(src_slice, src_buf, st_i, sq_i, dest_fn, dT_writes):
            so = st_i * 12
            s_sum = stat[:, so + 0:so + 1]
            s_sq = stat[:, so + 1:so + 2]
            s_mu = stat[:, so + 2:so + 3]
            s_sq2 = stat[:, so + 3:so + 4]
            s_var = stat[:, so + 4:so + 5]
            s_r = stat[:, so + 5:so + 6]
            s_nb = stat[:, so + 6:so + 7]
            sb = B_lnst[st_i]
            sched.op("v", lambda v: v.reduce_sum(s_sum, src_slice, axis=AX.X),
                     reads=[src_buf], writes=[sb])
            sq_v = sqscr2l[sq_i]
            sq_b = B_sqscr[sq_i]
            sched.op("s", lambda s: s.activation(sq_v, src_slice, AF.Square),
                     reads=[src_buf], writes=[sq_b])
            sched.op("v", lambda v: v.reduce_sum(s_sq, sq_v, axis=AX.X),
                     reads=[sq_b], writes=[sb])
            sched.op("v", lambda v: v.tensor_scalar_mul(s_mu, s_sum, 1.0 / 2048.0),
                     reads=[sb], writes=[sb])
            sched.op("v", lambda v: v.tensor_scalar_mul(s_sq2, s_sq, 1.0 / 2048.0),
                     reads=[sb], writes=[sb])
            sched.op("v", lambda v: v.tensor_mul(s_var, s_mu, s_mu),
                     reads=[sb], writes=[sb])
            sched.op("v", lambda v: v.tensor_sub(s_var, s_sq2, s_var),
                     reads=[sb], writes=[sb])
            sba = B_lnact[st_i]
            sched.op("s", lambda s: s.activation(s_var, s_var, AF.Ln, bias=c_eps5),
                     reads=[sb, B_const], writes=[sba])
            sched.op("s", lambda s: s.activation(s_r, s_var, AF.Exp, scale=c_negh),
                     reads=[sba, B_const], writes=[sba])
            sched.op("v", lambda v: v.scalar_tensor_tensor(
                s_nb, s_mu, -1.0, s_r, ALU.mult, ALU.mult),
                reads=[sb, sba], writes=[sb])
            xst = xnstage[sq_i]
            xstb = B_xnst[sq_i]
            sched.op("v", lambda v: v.tensor_scalar(xst, src_slice, s_r, s_nb, ALU.mult, ALU.add),
                     reads=[src_buf, sb, sba], writes=[xstb])
            for grp2 in range(2):
                def tr_fn(t, grp2=grp2, xst=xst):
                    ins = None
                    for k in range(8):
                        db = grp2 * 8 + k
                        ins = t.transpose(psT[:, k * 128:(k + 1) * 128],
                                          xst[:, db * 128:(db + 1) * 128], ident)
                    return ins
                sched.op("t", tr_fn, reads=[xstb, B_const], writes=[B_psT])
                dst = dest_fn(grp2)
                sched.op("v", lambda v, dst=dst: v.tensor_copy(dst, psT),
                         reads=[B_psT], writes=dT_writes)

        for pi in range(4):
            s_idx, H = divmod(pi, 2)
            xbase = s_idx * D * D

            lg_all = gst[:, 0:64]
            eb_all = gst[:, 64:128]
            mx_all = stat[:, 0:8]
            sm_all = stat[:, 8:16]
            rc_all = stat[:, 16:24]
            mm_all = stat[:, 24:32]

            def emit_acc_tile(grp, ti):
                accs = acc[:, grp * D:(grp + 1) * D]
                off = xbase + (1024 * H + 2 * grp) * D
                xap = bass.AP(x_h, off, [[D, 2], [16 * D, 64], [1, D]])
                sched.op("q", lambda q, accs=accs, xap=xap: q.dma_start(out=accs, in_=xap),
                         writes=[B_acc[grp]])

                def dest_fn(grp2, grp=grp):
                    return bass.AP(xnT.tensor, grp2 * 8 * 1024 + grp * 128,
                                   [[DB * 1024, P], [1024, 8], [1, P]])
                ln_tile(accs, B_acc[grp], ti % 4, ti % 2, dest_fn, [B_xnT, B_xnt[grp]])

            def emit_xg_tile(grp, ti):
                stg = stage_f32[grp % 2]
                off = xbase + (128 * 2 * grp + 64 * H) * D
                xap = bass.AP(x_h, off, [[128 * D, 2], [D, 64], [1, D]])
                sched.op("g", lambda g, stg=stg, xap=xap: g.dma_start(out=stg, in_=xap),
                         writes=[B_stage[grp % 2]])

                def dest_fn(grp2, grp=grp):
                    return bass.AP(xgom.tensor, grp2 * 8 * 1024 + grp * 128,
                                   [[DB * 1024, P], [1024, 8], [1, P]])
                ln_tile(stg, B_stage[grp % 2], ti % 4, ti % 2, dest_fn, [B_xg, B_xgt[grp]])

            def emit_S(tt):
                def s_chain(t, tt=tt):
                    ins = None
                    for db in range(DB):
                        ins = t.matmul(psD[0:NH, :],
                                       ws_sb[:, db * NH:(db + 1) * NH],
                                       xnT[:, db * 1024 + tt * 512: db * 1024 + tt * 512 + 512],
                                       start=(db == 0), stop=(db == DB - 1))
                    return ins
                sched.op("t", s_chain,
                         reads=[B_xnt[4 * tt + i] for i in range(4)] + [B_const],
                         writes=[B_psD])
                dst = sT[:, tt * 512:(tt + 1) * 512]
                sched.op("s", lambda s, dst=dst: s.activation(dst, psD[0:NH, :], AF.Silu, bias=bs_sb),
                         reads=[B_psD, B_const], writes=[B_sT])

            def emit_G_chain(ab):
                def g_chain(t, ab=ab):
                    ins = None
                    for db in range(DB):
                        ins = t.matmul(psD[:, 0:EXP],
                                       xnT[:, db * 1024 + ab * 128: db * 1024 + (ab + 1) * 128],
                                       gw_sb[:, db * EXP:(db + 1) * EXP],
                                       start=(db == 0), stop=(db == DB - 1))
                    return ins
                sched.op("t", g_chain, reads=[B_xnt[ab], B_const], writes=[B_psD])
                dstl = lg_all[:, ab * 8:(ab + 1) * 8]
                sched.op("v", lambda v, dstl=dstl: v.tensor_copy(dstl, psD[:, 0:EXP]),
                         reads=[B_psD], writes=[B_gst])

            ti = 0
            for grp in range(8):
                emit_acc_tile(grp, ti); ti += 1
                emit_xg_tile(grp, ti); ti += 1
                if grp == 3:
                    emit_S(0)
                    for ab in range(4):
                        emit_G_chain(ab)
                if grp == 7:
                    emit_S(1)
                    for ab in range(4, 8):
                        emit_G_chain(ab)

            for hh in range(NH):
                ub = hh % 3
                sched.op("g", lambda g, ub=ub, hh=hh: g.dma_start(out=upb[ub], in_=wq_h.ap()[hh]),
                         writes=[B_up[ub]])
                for tt in range(2):
                    pst, psb_ = ps_halves[(hh * 2 + tt) % 4]

                    def q_chain(t, ub=ub, tt=tt, pst=pst):
                        ins = None
                        for db in range(DB):
                            ins = t.matmul(pst, upb[ub][:, db * 128:(db + 1) * 128],
                                           xgom[:, db * 1024 + tt * 512: db * 1024 + tt * 512 + 512],
                                           start=(db == 0), stop=(db == DB - 1))
                        return ins
                    sched.op("t", q_chain,
                             reads=[B_up[ub]] + [B_xgt[4 * tt + i] for i in range(4)],
                             writes=[psb_], war_reads=[B_xg])
                    dst = bass.AP(gq.tensor, 8 * tt * 1024 + hh * 64,
                                  [[2 * HBE * 1024, P], [1024, 8], [1, 64]])
                    qn_w = [B_qn[qb] for qb in range(8 * tt, 8 * tt + 8)]
                    sched.op("s", lambda s, dst=dst, pst=pst: s.activation(dst, pst, AF.Silu),
                             reads=[psb_], writes=qn_w)

            for ab in range(8):
                sched.op("v", lambda v, ab=ab: v.reduce_max(
                    mx_all[:, ab:ab + 1], lg_all[:, ab * 8:(ab + 1) * 8], axis=AX.X),
                    reads=[B_gst], writes=[B_stat[0]])
            for ab in range(8):
                sched.op("v", lambda v, ab=ab: v.tensor_scalar_sub(
                    eb_all[:, ab * 8:(ab + 1) * 8], lg_all[:, ab * 8:(ab + 1) * 8],
                    mx_all[:, ab:ab + 1]),
                    reads=[B_gst, B_stat[0]], writes=[B_gst])
            sched.op("s", lambda s: s.activation(lg_all, eb_all, AF.Exp),
                     reads=[B_gst], writes=[B_gst])
            for ab in range(8):
                sched.op("v", lambda v, ab=ab: v.reduce_sum(
                    sm_all[:, ab:ab + 1], lg_all[:, ab * 8:(ab + 1) * 8], axis=AX.X),
                    reads=[B_gst], writes=[B_stat[0]])
            sched.op("v", lambda v: v.reciprocal(rc_all, sm_all),
                     reads=[B_stat[0]], writes=[B_stat[0]])
            for ab in range(8):
                pp = comb[:, ab * EXP:(ab + 1) * EXP]
                sched.op("v", lambda v, pp=pp, ab=ab: v.tensor_scalar_mul(
                    pp, lg_all[:, ab * 8:(ab + 1) * 8], rc_all[:, ab:ab + 1]),
                    reads=[B_gst, B_stat[0]], writes=[B_comb])
            for ab in range(8):
                pp = comb[:, ab * EXP:(ab + 1) * EXP]
                sched.op("v", lambda v, pp=pp, ab=ab: v.reduce_max(
                    mm_all[:, ab:ab + 1], pp, axis=AX.X),
                    reads=[B_comb], writes=[B_stat[0]])
            for ab in range(8):
                pp = comb[:, ab * EXP:(ab + 1) * EXP]
                ee = eb_all[:, ab * 8:(ab + 1) * 8]
                sched.op("v", lambda v, pp=pp, ee=ee, ab=ab: v.tensor_scalar(
                    ee, pp, mm_all[:, ab:ab + 1], None, ALU.is_ge),
                    reads=[B_comb, B_stat[0]], writes=[B_gst])
            for ab in range(8):
                pp = comb[:, ab * EXP:(ab + 1) * EXP]
                ee = eb_all[:, ab * 8:(ab + 1) * 8]
                sched.op("v", lambda v, pp=pp, ee=ee: v.scalar_tensor_tensor(
                    ee, ee, -1e30, pp, ALU.mult, ALU.add),
                    reads=[B_comb, B_gst], writes=[B_gst])
            for ab in range(8):
                ee = eb_all[:, ab * 8:(ab + 1) * 8]
                sched.op("v", lambda v, ee=ee, ab=ab: v.reduce_max(
                    mm_all[:, ab:ab + 1], ee, axis=AX.X),
                    reads=[B_gst], writes=[B_stat[0]])
            for ab in range(8):
                pp = comb[:, ab * EXP:(ab + 1) * EXP]
                ee = eb_all[:, ab * 8:(ab + 1) * 8]
                sched.op("v", lambda v, pp=pp, ee=ee, ab=ab: v.tensor_scalar(
                    ee, pp, mm_all[:, ab:ab + 1], None, ALU.is_ge),
                    reads=[B_comb, B_stat[0]], writes=[B_gst])
            for ab in range(8):
                pp = comb[:, ab * EXP:(ab + 1) * EXP]
                ee = eb_all[:, ab * 8:(ab + 1) * 8]
                sched.op("v", lambda v, pp=pp, ee=ee: v.tensor_mul(pp, pp, ee),
                         reads=[B_gst, B_comb], writes=[B_comb])

            if "dump" in _DBG and pi == _DPI:
                sched.op("q", lambda q: q.dma_start(out=dxnT_h.ap(), in_=xnT),
                         reads=[B_xnT])
                sched.op("q", lambda q: q.dma_start(out=dxgT_h.ap(), in_=xgom[:, 0:16384]),
                         reads=[B_xg])
                sched.op("q", lambda q: q.dma_start(out=dcomb_h.ap(), in_=comb),
                         reads=[B_comb])
                sched.op("q", lambda q: q.dma_start(out=dsT_h.ap(), in_=sT),
                         reads=[B_sT])
                sched.op("q", lambda q: q.dma_start(out=dqp_h.ap(), in_=qn_all),
                         reads=B_qn)

            for q in range(NH):
                qs = qn_all[:, q * 1024:(q + 1) * 1024]
                q2 = xgom[:, q * 1024:(q + 1) * 1024]
                sched.op("v", lambda v, q2=q2, qs=qs: v.tensor_mul(q2, qs, qs),
                         reads=[B_qn[q]], writes=[B_om[q]])

            def bf_slot(q):
                t, idx = q // 6, q % 6
                r = 32 * (idx % 3)
                c = (idx // 3) * 1024
                return upb[2 - t][r:r + 1, c:c + 1024], r

            def a2a_norm(q):
                k, r = q % 6, 32 * (q % 3)
                X = psA if (q // 3) % 2 == 0 else psB
                q2 = xgom[:, q * 1024:(q + 1) * 1024]
                row = X[r:r + 1, 0:1024]

                def ss_mm(t, X=X, r=r, q2=q2):
                    t.matmul(X[r:r + 1, 0:512], ones_col, q2[:, 0:512],
                             start=True, stop=True)
                    return t.matmul(X[r:r + 1, 512:1024], ones_col, q2[:, 512:1024],
                                    start=True, stop=True)
                sched.op("t", ss_mm, reads=[B_om[q], B_const], writes=[B_sl[k]])
                sched.op("s", lambda s, row=row: s.activation(row, row, AF.Ln),
                         reads=[B_sl[k]], writes=[B_sl[k]])
                sched.op("s", lambda s, row=row, r=r: s.activation(
                    row, row, AF.Exp, scale=c_half[r:r + 1, :]),
                    reads=[B_sl[k], B_const], writes=[B_sl[k]])
                sched.op("s", lambda s, row=row, r=r: s.activation(
                    row, row, AF.Ln, bias=c_eps5[r:r + 1, :]),
                    reads=[B_sl[k], B_const], writes=[B_sl[k]])
                bf, _r = bf_slot(q)
                sched.op("s", lambda s, bf=bf, row=row, r=r: s.activation(
                    bf, row, AF.Exp, scale=c_neg1[r:r + 1, :]),
                    reads=[B_sl[k], B_const], writes=[B_bfq[q]])

            def a2a_bcast(q):
                qs = qn_all[:, q * 1024:(q + 1) * 1024]
                bf, r = bf_slot(q)

                def rb_mm(t, bf=bf, r=r):
                    t.matmul(psC[:, 0:512], ones_blk[r:r + 1, :], bf[:, 0:512],
                             start=True, stop=True)
                    return t.matmul(psC[:, 512:1024], ones_blk[r:r + 1, :],
                                    bf[:, 512:1024], start=True, stop=True)
                sched.op("t", rb_mm, reads=[B_bfq[q], B_const],
                         writes=[B_psC0, B_psC1])
                sched.op("v", lambda v, qs=qs: v.tensor_mul(qs, qs, psC),
                         reads=[B_psC0, B_psC1, B_qn[q]], writes=[B_qn[q]])

            for q in range(NH):
                a2a_norm(q)
                if q >= 3:
                    a2a_bcast(q - 3)
            for q in range(NH - 3, NH):
                a2a_bcast(q)

            B_psCq = [sched.buf(f"psCq{i}", "psC", i * 128, (i + 1) * 128)
                      for i in range(8)]
            B_psBq = [sched.buf(f"psBq{i}", "psB", i * 128, (i + 1) * 128)
                      for i in range(8)]

            def o_ps(qb):
                if qb % 2 == 0:
                    return psC, B_psCq, B_psC0, B_psC1
                return psB, B_psBq, B_psB0, B_psB1
            B_pstq = [sched.buf(f"pstq{i}", "psT", i * 128, (i + 1) * 128)
                      for i in range(8)]
            B_z = sched.buf("zbuf", "upb1", 0, 1024)
            B_y = sched.buf("ybuf", "upb1", 1024, 2048)
            B_stM = sched.buf("stM", "stat", 0, 8, small=True)
            B_stV = sched.buf("stV", "stat", 8, 16, small=True)
            B_stQ = sched.buf("stQ", "stat", 16, 24, small=True)
            B_o2 = [sched.buf("o2a", "gq", 16384, 16640, small=True),
                    sched.buf("o2b", "gq", 16640, 16896, small=True)]
            o2sc = [gq_f32[:, 8192:8320], gq_f32[:, 8320:8448]]
            B_oc = [sched.buf("oca", "gq", 16896, 17152, small=True),
                    sched.buf("ocb", "gq", 17152, 17408, small=True)]
            ocsc = [gq_f32[:, 8448:8576], gq_f32[:, 8576:8704]]
            zt = upb[1][:, 0:1024]
            yt = upb[1][:, 1024:2048]
            m8, v8, q8 = stat[:, 0:8], stat[:, 8:16], stat[:, 16:24]
            srows = [srow_bf, gq[0:1, 20480:21504]]
            B_srows = [B_rowS, B_rowB]

            def a2_tail(qb):
                oms = xgom[:, qb * 1024:(qb + 1) * 1024]
                for i in range(8):
                    def tr_mm(t, i=i):
                        return t.transpose(psT[:, i * 128:(i + 1) * 128],
                                           zt[:, i * 128:(i + 1) * 128], ident)
                    sched.op("t", tr_mm, reads=[B_z, B_const],
                             writes=[B_pstq[i], B_psT])
                for i in range(8):
                    sched.op("v", lambda v, i=i: v.tensor_scalar(
                        yt[:, i * 128:(i + 1) * 128], psT[:, i * 128:(i + 1) * 128],
                        lng, lnb, ALU.mult, ALU.add),
                        reads=[B_pstq[i], B_const], writes=[B_y])

                def sb0_mm(t, qb=qb):
                    return t.matmul(psD, ones_row, srows[qb % 2][:, 0:512],
                                    start=True, stop=True)
                sched.op("t", sb0_mm, reads=[B_srows[qb % 2], B_const], writes=[B_psD])
                sched.op("v", lambda v, oms=oms: v.tensor_mul(
                    oms[:, 0:512], psD, yt[:, 0:512]),
                    reads=[B_psD, B_y], writes=[B_om[qb]])

                X, Bq4, _b0, Xb1 = o_ps(qb)

                def sb1_mm(t, X=X, qb=qb):
                    return t.matmul(X[:, 512:1024], ones_row,
                                    srows[qb % 2][:, 512:1024], start=True, stop=True)
                sched.op("t", sb1_mm, reads=[B_srows[qb % 2], B_const],
                         writes=[Bq4[4], Bq4[5], Bq4[6], Bq4[7], Xb1])
                sched.op("v", lambda v, oms=oms, X=X: v.tensor_mul(
                    oms[:, 512:1024], X[:, 512:1024], yt[:, 512:1024]),
                    reads=[Bq4[4], Bq4[5], Bq4[6], Bq4[7], B_y],
                    writes=[B_om[qb]])

            for qb in range(NH):
                qs = qn_all[:, qb * 1024:(qb + 1) * 1024]
                X, BXq, Xb0, Xb1 = o_ps(qb)

                def sx_mm(t, qb=qb):
                    sel = ident[0:NH, qb:qb + 1]
                    t.matmul(psA[0:1, 0:512], sel, sT[:, 0:512], start=True, stop=True)
                    return t.matmul(psA[0:1, 512:1024], sel, sT[:, 512:1024],
                                    start=True, stop=True)
                sched.op("t", sx_mm, reads=[B_sT, B_const], writes=[B_psA0, B_psA1])
                sched.op("s", lambda s, qb=qb: s.activation(
                    srows[qb % 2], psA[0:1, 0:1024], AF.Copy),
                    reads=[B_psA0, B_psA1], writes=[B_srows[qb % 2]])

                def ab_half(h, qb=qb, qs=qs):
                    def mm(t, h=h, qb=qb, qs=qs):
                        w0l = w012[:, qb * 128:(qb + 1) * 128]
                        w2l = w012[:, 2 * 2048 + qb * 128: 2 * 2048 + (qb + 1) * 128]
                        t.matmul(psA[:, 0:512], w0l, qs[:, h * 512:(h + 1) * 512],
                                 start=True, stop=True)
                        return t.matmul(psA[:, 512:1024], w2l,
                                        qs[:, h * 512:(h + 1) * 512],
                                        start=True, stop=True)
                    sched.op("t", mm, reads=[B_qn[qb], B_const],
                             writes=[B_psA0, B_psA1])
                    sched.op("s", lambda s, h=h: s.activation(
                        qsq[:, h * 512:(h + 1) * 512], psA[:, 0:512], AF.Silu),
                        reads=[B_psA0], writes=[B_scr0])
                    sched.op("v", lambda v, h=h: v.tensor_mul(
                        hid[:, h * 512:(h + 1) * 512], psA[:, 512:1024],
                        qsq[:, h * 512:(h + 1) * 512]),
                        reads=[B_psA1, B_scr0], writes=[B_scr0])

                ab_half(0)
                if qb > 0:
                    a2_tail(qb - 1)
                ab_half(1)

                for i in range(8):
                    def o_mm(t, qb=qb, i=i, X=X):
                        w1l = w012[:, 2048 + qb * 128: 2048 + (qb + 1) * 128]
                        return t.matmul(X[:, i * 128:(i + 1) * 128],
                                        hid[:, i * 128:(i + 1) * 128], w1l,
                                        start=True, stop=True)
                    sched.op("t", o_mm, reads=[B_scr0, B_const],
                             writes=[BXq[i], Xb0 if i < 4 else Xb1])
                for i in range(8):
                    pq = X[:, i * 128:(i + 1) * 128]
                    sched.op("s", lambda s, pq=pq, i=i: s.activation(
                        ocsc[i % 2], pq, AF.Copy),
                        reads=[BXq[i]], writes=[B_oc[i % 2]])
                    sched.op("v", lambda v, i=i: v.reduce_sum(
                        m8[:, i:i + 1], ocsc[i % 2], axis=AX.X),
                        reads=[B_oc[i % 2]], writes=[B_stM])
                    sched.op("s", lambda s, pq=pq, i=i: s.activation(
                        o2sc[i % 2], pq, AF.Square),
                        reads=[BXq[i]], writes=[B_o2[i % 2]])
                    sched.op("v", lambda v, i=i: v.reduce_sum(
                        v8[:, i:i + 1], o2sc[i % 2], axis=AX.X),
                        reads=[B_o2[i % 2]], writes=[B_stV])
                sched.op("v", lambda v: v.tensor_scalar_mul(m8, m8, 1.0 / 128.0),
                         reads=[B_stM], writes=[B_stM])
                sched.op("v", lambda v: v.tensor_mul(q8, m8, m8),
                         reads=[B_stM], writes=[B_stQ])
                sched.op("v", lambda v: v.scalar_tensor_tensor(
                    v8, v8, 1.0 / 128.0, q8, ALU.mult, ALU.subtract),
                    reads=[B_stV, B_stQ], writes=[B_stV])
                sched.op("s", lambda s: s.activation(v8, v8, AF.Ln, bias=c_eps6),
                         reads=[B_stV, B_const], writes=[B_stV])
                sched.op("s", lambda s: s.activation(v8, v8, AF.Exp, scale=c_negh),
                         reads=[B_stV, B_const], writes=[B_stV])
                for i in range(8):
                    pq = X[:, i * 128:(i + 1) * 128]
                    sched.op("v", lambda v, pq=pq, i=i: v.tensor_scalar(
                        zt[:, i * 128:(i + 1) * 128], pq, m8[:, i:i + 1],
                        v8[:, i:i + 1], ALU.subtract, ALU.mult),
                        reads=[BXq[i], B_stM, B_stV], writes=[B_z])
            a2_tail(NH - 1)

            if "dump" in _DBG and pi == _DPI:
                sched.op("q", lambda q: q.dma_start(out=dom_h.ap(), in_=xgom),
                         reads=B_om)

            for dq in range(4):
                wb = dq % 2
                sched.op("g", lambda g, wb=wb, dq=dq: g.dma_start(out=wo_chunk[wb], in_=wo_h.ap()[dq]),
                         writes=[B_wo[wb]])
                for ab in range(8):
                    pst, psb_ = ps4[(dq * 8 + ab) % 4]

                    def wo_chain(t, wb=wb, ab=ab, pst=pst):
                        ins = None
                        for ffb in range(DB):
                            ins = t.matmul(pst,
                                           xgom[:, ffb * 1024 + ab * 128: ffb * 1024 + (ab + 1) * 128],
                                           wo_chunk[wb][:, ffb * 512:(ffb + 1) * 512],
                                           start=(ffb == 0), stop=(ffb == DB - 1))
                        return ins
                    sched.op("t", wo_chain, reads=[B_wo[wb]] + B_om, writes=[psb_])
                    accs = acc[:, ab * D + dq * 512: ab * D + (dq + 1) * 512]
                    if "noattn" not in _DBG:
                        sched.op("v", lambda v, accs=accs, pst=pst: v.tensor_add(accs, pst, accs),
                                 reads=[psb_, B_acc[ab]], writes=[B_acc[ab]])

            def up_phase(wsrc_g, wsrc_u, hbn, gbs, gbuf):
                for hb in range(hbn):
                    bg = (2 * hb) % 3
                    bu = (2 * hb + 1) % 3
                    sched.op("g", lambda g, bg=bg, hb=hb, wsrc_g=wsrc_g:
                             g.dma_start(out=upb[bg], in_=wsrc_g[hb]),
                             writes=[B_up[bg]])
                    sched.op("g", lambda g, bu=bu, hb=hb, wsrc_u=wsrc_u:
                             g.dma_start(out=upb[bu], in_=wsrc_u[hb]),
                             writes=[B_up[bu]])
                    for tt in range(2):
                        pg = psA[:, tt * 512:(tt + 1) * 512]
                        pgb = [B_psA0, B_psA1][tt]

                        def gate_chain(t, bg=bg, tt=tt, pg=pg):
                            ins = None
                            for db in range(DB):
                                ins = t.matmul(pg, upb[bg][:, db * 128:(db + 1) * 128],
                                               xnT[:, db * 1024 + tt * 512: db * 1024 + tt * 512 + 512],
                                               start=(db == 0), stop=(db == DB - 1))
                            return ins
                        sched.op("t", gate_chain,
                                 reads=[B_up[bg]] + [B_xnt[4 * tt + i] for i in range(4)],
                                 writes=[pgb])
                        sv = sil_f32[tt]
                        svb = B_sil[tt]
                        sched.op("s", lambda s, sv=sv, pg=pg: s.activation(sv, pg, AF.Silu),
                                 reads=[pgb], writes=[svb])
                    for tt in range(2):
                        pu = psB[:, tt * 512:(tt + 1) * 512]
                        pub = [B_psB0, B_psB1][tt]

                        def up_chain(t, bu=bu, tt=tt, pu=pu):
                            ins = None
                            for db in range(DB):
                                ins = t.matmul(pu, upb[bu][:, db * 128:(db + 1) * 128],
                                               xnT[:, db * 1024 + tt * 512: db * 1024 + tt * 512 + 512],
                                               start=(db == 0), stop=(db == DB - 1))
                            return ins
                        sched.op("t", up_chain,
                                 reads=[B_up[bu]] + [B_xnt[4 * tt + i] for i in range(4)],
                                 writes=[pub])
                        dstg = gbuf[:, hb * 1024 + tt * 512: hb * 1024 + tt * 512 + 512]
                        sv = sil_f32[tt]
                        svb = B_sil[tt]
                        sched.op("v", lambda v, dstg=dstg, pu=pu, sv=sv: v.tensor_mul(dstg, pu, sv),
                                 reads=[pub, svb], writes=gbs)

            def down_phase(gbs, gbuf, wsrc_list, nhb, colw, ev_comb):
                for ci, wap in enumerate(wsrc_list):
                    cb = ci % 2
                    sched.op("g", lambda g, cb=cb, wap=wap, nhb=nhb, colw=colw: g.dma_start(
                        out=wd_chunk[cb][:, 0:nhb * colw], in_=wap),
                        writes=[B_dch[cb]])
                    for ab in range(8):
                        pst, psb_ = ps4[(ci * 8 + ab) % 4]
                        pslc = pst[:, 0:colw]

                        def d_chain(t, cb=cb, ab=ab, pslc=pslc, nhb=nhb, colw=colw):
                            ins = None
                            for hb in range(nhb):
                                ins = t.matmul(pslc,
                                               gbuf[:, hb * 1024 + ab * 128: hb * 1024 + (ab + 1) * 128],
                                               wd_chunk[cb][:, hb * colw:(hb + 1) * colw],
                                               start=(hb == 0), stop=(hb == nhb - 1))
                            return ins
                        sched.op("t", d_chain, reads=[B_dch[cb]] + gbs, writes=[psb_])
                        accs = acc[:, ab * D + ci * colw: ab * D + ci * colw + colw]
                        if ev_comb is not None:
                            if "nomoe" in _DBG:
                                continue
                            cc = comb[:, ab * EXP + ev_comb: ab * EXP + ev_comb + 1]
                            sched.op("v", lambda v, accs=accs, pslc=pslc, cc=cc:
                                     v.scalar_tensor_tensor(accs, pslc, cc, accs, ALU.mult, ALU.add),
                                     reads=[psb_, B_comb, B_acc[ab]], writes=[B_acc[ab]])
                        else:
                            if "nosh" in _DBG:
                                continue
                            sched.op("v", lambda v, accs=accs, pslc=pslc:
                                     v.tensor_add(accs, pslc, accs),
                                     reads=[psb_, B_acc[ab]], writes=[B_acc[ab]])

            for ei in range(EXP):
                gbs = [B_g[ei % 2]]
                gbuf = g_buf[ei % 2]
                up_phase(wg_h.ap()[ei], wu_h.ap()[ei], HBE, gbs, gbuf)
                if "dump" in _DBG and pi == 0 and ei == 0:
                    sched.op("q", lambda q: q.dma_start(out=dg_h.ap(), in_=g_buf[0]),
                             reads=[B_g[0]])
                down_phase(gbs, gbuf, [wd_h.ap()[ei][dq] for dq in range(4)], HBE, 512, ei)

            up_phase(swg_h.ap(), swu_h.ap(), HBS, [B_g[0], B_g[1]], gs_full)
            down_phase([B_g[0], B_g[1]], gs_full, [swd_h.ap()[pc] for pc in range(8)],
                       HBS, 256, None)

            for grp in range(8):
                accs = acc[:, grp * D:(grp + 1) * D]
                off = xbase + (1024 * H + 2 * grp) * D
                oap = bass.AP(out_h, off, [[D, 2], [16 * D, 64], [1, D]])
                sched.op("q", lambda q, accs=accs, oap=oap: q.dma_start(out=oap, in_=accs),
                         reads=[B_acc[grp]])

        for k in sched.semkeys["q"]:
            if sched.count[k]:
                sched.wait_only("q", (k, sched.count[k]))

        block = e(nc.Block())

        @block.sync
        def _(sync):
            sched.run("q", sync)

        @block.gpsimd
        def _(gp):
            sched.run("g", gp)

        @block.tensor
        def _(t):
            sched.run("t", t)

        @block.vector
        def _(v):
            sched.run("v", v)

        @block.scalar
        def _(s):
            sched.run("s", s)

    return nc


def _relayout(inputs):
    f = np.float32
    Wqkv = np.asarray(inputs["L0_attn_W_qkv"], f)
    qcols = (384 * np.arange(16)[:, None] + np.arange(128)[None, :]).reshape(-1)
    Wq = np.ascontiguousarray(Wqkv[:, qcols])
    wq = np.ascontiguousarray(Wq.reshape(16, 128, 16, 128).transpose(2, 1, 0, 3)
                              .reshape(NH, P, DB * P))
    Wo = np.asarray(inputs["L0_attn_W_o"], f)
    wo = np.ascontiguousarray(Wo.reshape(16, 128, 4, 512).transpose(2, 1, 0, 3)
                              .reshape(4, P, DB * 512))
    Ws = np.asarray(inputs["L0_attn_W_s"], f)
    ws = np.ascontiguousarray(Ws.reshape(16, 128, 16).transpose(1, 0, 2).reshape(P, DB * NH))
    gwv = np.asarray(inputs["L0_ffn_gate_w"], f)
    gw = np.ascontiguousarray(gwv.reshape(16, 128, 8).transpose(1, 0, 2).reshape(P, DB * EXP))
    w0 = np.asarray(inputs["L0_attn_w0"], f)
    w1 = np.asarray(inputs["L0_attn_w1"], f)
    w2 = np.asarray(inputs["L0_attn_w2"], f)

    def wt(w):
        return w.transpose(2, 0, 1).reshape(P, NH * P)
    w012 = np.ascontiguousarray(np.concatenate([wt(w0), wt(w1), wt(w2)], axis=1))
    wg = np.asarray(inputs["L0_ffn_w_gate"], f)
    wgr = np.ascontiguousarray(wg.reshape(8, 16, 128, 11, 128).transpose(0, 3, 2, 1, 4)
                               .reshape(EXP, HBE, P, DB * P))
    wu = np.asarray(inputs["L0_ffn_w_up"], f)
    wur = np.ascontiguousarray(wu.reshape(8, 16, 128, 11, 128).transpose(0, 3, 2, 1, 4)
                               .reshape(EXP, HBE, P, DB * P))
    wd = np.asarray(inputs["L0_ffn_w_down"], f)
    wdr = np.ascontiguousarray(wd.reshape(8, 11, 128, 4, 512).transpose(0, 3, 2, 1, 4)
                               .reshape(EXP, 4, P, HBE * 512))
    swg = np.asarray(inputs["L0_ffn_sw_gate"], f)
    swgr = np.ascontiguousarray(swg.reshape(16, 128, 22, 128).transpose(2, 1, 0, 3)
                                .reshape(HBS, P, DB * P))
    swu = np.asarray(inputs["L0_ffn_sw_up"], f)
    swur = np.ascontiguousarray(swu.reshape(16, 128, 22, 128).transpose(2, 1, 0, 3)
                                .reshape(HBS, P, DB * P))
    swd = np.asarray(inputs["L0_ffn_sw_down"], f)
    swdr = np.ascontiguousarray(swd.reshape(22, 128, 8, 256).transpose(2, 1, 0, 3)
                                .reshape(8, P, HBS * 256))
    return dict(
        wq=wq, wo=wo, ws=ws, gw=gw,
        bs=np.ascontiguousarray(np.asarray(inputs["L0_attn_b_s"], f).reshape(NH, 1)),
        lng=np.ascontiguousarray(np.asarray(inputs["L0_attn_ln_g"], f).reshape(P, 1)),
        lnb=np.ascontiguousarray(np.asarray(inputs["L0_attn_ln_b"], f).reshape(P, 1)),
        w012=w012, wg=wgr, wu=wur, wd=wdr, swg=swgr, swu=swur, swd=swdr,
        ident=np.eye(P, dtype=BFD),
    )


def kernel(**inputs):
    weights = _relayout(inputs)
    x = np.asarray(inputs["L0_attn_x"], np.float32)
    nc = build_program()
    in_maps = []
    for c in range(N_CORES):
        m = dict(weights)
        m["x"] = np.ascontiguousarray(x[SEQ_PER_CORE * c: SEQ_PER_CORE * (c + 1)])
        in_maps.append(m)
    res = run_bass_kernel_spmd(nc, in_maps, core_ids=list(range(N_CORES)))
    return np.concatenate([np.asarray(r["out"]) for r in res.results], axis=0)


# version 29
# speedup vs baseline: 1.0512x; 1.0512x over previous
import os
import numpy as np
import ml_dtypes
from contextlib import ExitStack

_DBG = os.environ.get("KDBG", "")
_DPI = int(os.environ.get("KDPI", "0"))

import concourse.bass as bass
import concourse.mybir as mybir
from concourse.bass_utils import run_bass_kernel_spmd

F32 = mybir.dt.float32
BF16 = mybir.dt.bfloat16
AF = mybir.ActivationFunctionType
ALU = mybir.AluOpType
AX = mybir.AxisListType
BFD = ml_dtypes.bfloat16

P = 128
D = 2048
DB = 16
NH = 16
EXP = 8
HBE = 11
HBS = 22
SEQ_PER_CORE = 2
N_CORES = 8


class Buf:
    __slots__ = ("name", "key", "lo", "hi", "w", "readers", "small")

    def __init__(self, name, key, lo, hi, small=False):
        self.name, self.key, self.lo, self.hi = name, key, lo, hi
        self.w = None
        self.readers = {}
        self.small = small


class Sched:

    N_DMA_LANES = 8

    def __init__(self):
        self.prog = {}
        self.count = {}
        self.water = {}
        self.inc_amt = {}
        self.semkeys = {}
        self.rot = {}
        self.sems = {}
        self.registry = {}

    SECTION = {"t": "t", "v": "v", "s": "s", "q": "q", "g": "g", "p": "g"}

    def add_engine(self, name, inc):
        sec = self.SECTION[name]
        if sec not in self.prog:
            self.prog[sec] = []
            self.water[sec] = {}
        self.inc_amt[name] = inc
        if inc == 16:
            self.semkeys[name] = [f"{name}{i}" for i in range(self.N_DMA_LANES)]
        else:
            self.semkeys[name] = [name]
        self.rot[name] = 0
        for k in self.semkeys[name]:
            self.count[k] = 0

    def buf(self, name, key, lo=0, hi=1 << 40, small=False):
        b = Buf(name, key, lo, hi, small)
        self.registry.setdefault(key, []).append(b)
        return b

    def op(self, en, fn, reads=(), writes=(), war_reads=()):
        waits = {}
        sec = self.SECTION[en]
        own = set()
        for e2, s2 in self.SECTION.items():
            if s2 == sec and e2 in self.rot:
                own.update(self.semkeys[e2])

        def need(tok, force_same=False):
            sem_key, val = tok
            if sem_key in own and sec != "g" and not force_same:
                return
            if sec != "g" and self.water[sec].get(sem_key, -1) >= val:
                return
            if waits.get(sem_key, -1) < val:
                waits[sem_key] = val

        eff_writes = []
        for b in writes:
            for o in self.registry.get(b.key, ()):
                if o.lo < b.hi and b.lo < o.hi:
                    eff_writes.append(o)
        for b in reads:
            if b.w:
                need(b.w, force_same=b.small)
        for b in eff_writes:
            if b.w:
                need(b.w, force_same=b.small)
            for tok in b.readers.values():
                need(tok, force_same=b.small)
        for k, v in waits.items():
            self.water[sec][k] = v
        inc = self.inc_amt[en]
        keys = self.semkeys[en]
        semkey = keys[self.rot[en] % len(keys)]
        self.rot[en] += 1
        if inc == 16 and self.count[semkey] > 0:
            if waits.get(semkey, -1) < self.count[semkey]:
                waits[semkey] = self.count[semkey]
        self.count[semkey] += inc
        tok = (semkey, self.count[semkey])
        for b in eff_writes:
            b.w = tok
            b.readers = {}
        for b in reads:
            b.readers[sec] = tok
        for b in war_reads:
            b.readers[sec] = tok
        self.prog[sec].append((sorted(waits.items()), fn, inc, semkey))
        return tok

    def wait_only(self, en, tok):
        self.prog[self.SECTION[en]].append(([tok], None, 0, None))

    def run(self, en, eng):
        for e2, s2 in self.SECTION.items():
            if s2 == en and e2 in self.rot:
                for k in self.semkeys[e2]:
                    eng.sem_clear(self.sems[k])
        for r in range(2):
            eng.sem_clear(self.boot[(en, r)])
        for r in range(2):
            eng.sem_inc(self.boot[(en, r)], 1)
            for other in self.prog:
                if other != en:
                    eng.wait_ge(self.boot[(other, r)], 1)
        for waits, fn, inc, semkey in self.prog[en]:
            for sem_key, val in waits:
                eng.wait_ge(self.sems[sem_key], val)
            if fn is not None:
                ins = fn(eng)
                if inc:
                    ins.then_inc(self.sems[semkey], inc)


def build_program():
    nc = bass.Bass("TRN2", target_bir_lowering=False)

    x_h = nc.dram_tensor("x", [SEQ_PER_CORE, D, D], F32, kind="ExternalInput")
    out_h = nc.dram_tensor("out", [SEQ_PER_CORE, D, D], F32, kind="ExternalOutput")
    wq_h = nc.dram_tensor("wq", [NH, P, DB * P], F32, kind="ExternalInput")
    wo_h = nc.dram_tensor("wo", [4, P, DB * 512], F32, kind="ExternalInput")
    ws_h = nc.dram_tensor("ws", [P, DB * NH], F32, kind="ExternalInput")
    gw_h = nc.dram_tensor("gw", [P, DB * EXP], F32, kind="ExternalInput")
    bs_h = nc.dram_tensor("bs", [NH, 1], F32, kind="ExternalInput")
    lng_h = nc.dram_tensor("lng", [P, 1], F32, kind="ExternalInput")
    lnb_h = nc.dram_tensor("lnb", [P, 1], F32, kind="ExternalInput")
    w012_h = nc.dram_tensor("w012", [P, 3 * NH * P], F32, kind="ExternalInput")
    wg_h = nc.dram_tensor("wg", [EXP, HBE, P, DB * P], F32, kind="ExternalInput")
    wu_h = nc.dram_tensor("wu", [EXP, HBE, P, DB * P], F32, kind="ExternalInput")
    wd_h = nc.dram_tensor("wd", [EXP, 4, P, HBE * 512], F32, kind="ExternalInput")
    swg_h = nc.dram_tensor("swg", [HBS, P, DB * P], F32, kind="ExternalInput")
    swu_h = nc.dram_tensor("swu", [HBS, P, DB * P], F32, kind="ExternalInput")
    swd_h = nc.dram_tensor("swd", [8, P, HBS * 256], F32, kind="ExternalInput")
    ident_h = nc.dram_tensor("ident", [P, P], BF16, kind="ExternalInput")
    if "dump" in _DBG:
        dxnT_h = nc.dram_tensor("dxnT", [P, DB * 1024], BF16, kind="ExternalOutput")
        dxgT_h = nc.dram_tensor("dxgT", [P, DB * 1024], BF16, kind="ExternalOutput")
        dcomb_h = nc.dram_tensor("dcomb", [P, 8 * EXP], F32, kind="ExternalOutput")
        dsT_h = nc.dram_tensor("dsT", [NH, 1024], BF16, kind="ExternalOutput")
        dqp_h = nc.dram_tensor("dqp", [P, DB * 1024], BF16, kind="ExternalOutput")
        dom_h = nc.dram_tensor("dom", [P, DB * 1024], BF16, kind="ExternalOutput")
        dg_h = nc.dram_tensor("dg", [P, HBE * 1024], BF16, kind="ExternalOutput")
        dgt_h = nc.dram_tensor("dgt", [P, 192], F32, kind="ExternalOutput")
        dst_h = nc.dram_tensor("dstat", [P, 48], F32, kind="ExternalOutput")

    sched = Sched()
    for en, inc in (("t", 1), ("v", 1), ("s", 1), ("q", 16), ("g", 16), ("p", 1)):
        sched.add_engine(en, inc)

    with ExitStack() as ctx:
        e = ctx.enter_context

        ident = e(nc.sbuf_tensor("identb", [P, P], BF16)).ap()
        dgt_sb = e(nc.sbuf_tensor("dgt_sb", [P, 8 * 24], F32)).ap() if "dump" in _DBG else None
        B_dgt = sched.buf("dgt", "dgt") if "dump" in _DBG else None
        ones_col = e(nc.sbuf_tensor("ones_col", [P, 1], BF16)).ap()
        ones_row = e(nc.sbuf_tensor("ones_row", [1, P], BF16)).ap()
        ones_blk = e(nc.sbuf_tensor("ones_blk", [P, P], BF16)).ap()
        w012 = e(nc.sbuf_tensor("w012b", [P, 3 * NH * P], BF16)).ap()
        ws_sb = e(nc.sbuf_tensor("ws_sb", [P, DB * NH], BF16)).ap()
        gw_sb = e(nc.sbuf_tensor("gw_sb", [P, DB * EXP], BF16)).ap()
        bs_sb = e(nc.sbuf_tensor("bs_sb", [NH, 1], F32)).ap()
        lng = e(nc.sbuf_tensor("lng_sb", [P, 1], F32)).ap()
        lnb = e(nc.sbuf_tensor("lnb_sb", [P, 1], F32)).ap()
        csb = e(nc.sbuf_tensor("csb", [P, 8], F32)).ap()
        acc = e(nc.sbuf_tensor("acc", [P, 8 * D], F32)).ap()
        xnT = e(nc.sbuf_tensor("xnT", [P, DB * 1024], BF16)).ap()
        xgom = e(nc.sbuf_tensor("xgom", [P, DB * 1024], BF16)).ap()
        gq = e(nc.sbuf_tensor("gqb", [P, 2 * HBE * 1024], BF16)).ap()
        upb = [e(nc.sbuf_tensor(f"upb{i}", [P, DB * P], BF16)).ap() for i in range(3)]
        sT = e(nc.sbuf_tensor("sT_sb", [NH, 1024], BF16)).ap()
        comb = e(nc.sbuf_tensor("comb_sb", [P, 8 * EXP], F32)).ap()
        stat = e(nc.sbuf_tensor("stat_sb", [P, 48], F32)).ap()
        gst = e(nc.sbuf_tensor("gst_sb", [P, 192], F32)).ap()
        psA = e(nc.psum_tensor("psA", [P, 1024], F32)).ap()
        psB = e(nc.psum_tensor("psB", [P, 1024], F32)).ap()
        psC = e(nc.psum_tensor("psC", [P, 1024], F32)).ap()
        psD = e(nc.psum_tensor("psD", [P, 512], F32)).ap()
        psT = e(nc.psum_tensor("psTb", [P, 1024], BF16)).ap()

        sched.boot = {}
        for en in ("t", "v", "s", "q", "g", "p"):
            for k in sched.semkeys[en]:
                sched.sems[k] = e(nc.semaphore(f"sem_{k}"))
        for en in ("t", "v", "s", "q", "g"):
            for r in range(2):
                sched.boot[(en, r)] = e(nc.semaphore(f"boot_{en}{r}"))

        gq_f32 = gq.bitcast(F32)
        stage_f32 = [gq_f32[:, 0:2048], gq_f32[:, 2048:4096]]
        xnstage = [gq[:, 8192:10240], gq[:, 10240:12288]]
        sqscr2l = [gq_f32[:, 6144:8192], gq_f32[:, 8192:10240]]
        qn_all = gq[:, 0:16384]
        wo_chunk = [gq[:, 0:8192], gq[:, 8192:16384]]
        g_buf = [gq[:, 0:11264], gq[:, 11264:22528]]
        gs_full = gq[:, 0:HBS * 1024]
        wd_chunk = [xgom[:, 0:5632], xgom[:, 5632:11264]]
        sil_f32 = [xgom.bitcast(F32)[:, 5632:6144], xgom.bitcast(F32)[:, 6144:6656]]
        qsq = upb[0][:, 0:1024]
        hid = upb[0][:, 1024:2048]
        O_sb = upb[1].bitcast(F32)[:, 0:1024]
        msq_row = upb[2].bitcast(F32)[0:1, 0:1024]
        r_row = gq_f32[0:1, 8192:9216]
        m_row = gq_f32[0:1, 9216:10240]
        brow = gq[0:1, 20480:21504]
        srow_bf = gq[0:1, 21504:22528]

        B_const = sched.buf("const", "const")
        B_acc = [sched.buf(f"acc{i}", "acc", i * D, (i + 1) * D) for i in range(8)]
        B_xnT = sched.buf("xnT", "xnT")
        B_xg = sched.buf("xg", "xgom", 0, 16384)
        B_xnt = [sched.buf(f"xnt{i}", "xnt", i, i + 1) for i in range(8)]
        B_xgt = [sched.buf(f"xgt{i}", "xgt", i, i + 1) for i in range(8)]
        B_sl = [sched.buf(f"sl{k}", "sslot", k, k + 1) for k in range(6)]
        B_bfq = [sched.buf(f"bfq{k}", "browf", k, k + 1) for k in range(NH)]
        B_q2p = [sched.buf(f"q2p{k}", "q2par", k, k + 1) for k in range(2)]
        B_om = [sched.buf(f"om{i}", "xgom", i * 1024, (i + 1) * 1024) for i in range(NH)]
        B_dch = [sched.buf("dch0", "xgom", 0, 5632), sched.buf("dch1", "xgom", 5632, 11264)]
        B_sil = [sched.buf("sil0", "xgom", 11264, 12288), sched.buf("sil1", "xgom", 12288, 13312)]
        B_stage = [sched.buf("stage0", "gq", 0, 4096), sched.buf("stage1", "gq", 4096, 8192)]
        B_xnst = [sched.buf("xnst0", "gq", 8192, 10240), sched.buf("xnst1", "gq", 10240, 12288)]
        B_sqscr = [sched.buf("sqscr0", "gq", 12288, 16384),
                   sched.buf("sqscr1", "gq", 16384, 20480)]
        B_qn = [sched.buf(f"qn{i}", "gq", i * 1024, (i + 1) * 1024) for i in range(NH)]
        B_wo = [sched.buf("wo0", "gq", 0, 8192), sched.buf("wo1", "gq", 8192, 16384)]
        B_g = [sched.buf("g0", "gq", 0, 11264), sched.buf("g1", "gq", 11264, 22528)]
        B_up = [sched.buf(f"up{i}", f"upb{i}") for i in range(3)]
        B_scr0 = sched.buf("scr0", "upb0")
        B_scr1 = sched.buf("scr1", "upb1")
        B_scr2 = sched.buf("scr2", "upb2")
        B_sT = sched.buf("sT", "sT")
        B_comb = sched.buf("comb", "comb", small=True)
        B_stat = [sched.buf("statA", "stat", 0, 24, small=True), sched.buf("statB", "stat", 24, 48, small=True)]
        B_lnst = [sched.buf(f"lnst{i}", "stat", i * 12, (i + 1) * 12, small=True) for i in range(4)]
        B_lnact = [sched.buf(f"lnact{i}", "stat", i * 12 + 4, i * 12 + 6, small=True) for i in range(4)]
        B_gst = sched.buf("gst", "gst", small=True)
        B_rowR = sched.buf("rowR", "gq", 16384, 18432, small=True)
        B_rowM = sched.buf("rowM", "gq", 18432, 20480, small=True)
        B_rowB = sched.buf("rowB", "gq", 20480, 21504, small=True)
        B_rowS = sched.buf("rowS", "gq", 21504, 22528, small=True)
        B_psA0 = sched.buf("psA0", "psA", 0, 512)
        B_psA1 = sched.buf("psA1", "psA", 512, 1024)
        B_psB0 = sched.buf("psB0", "psB", 0, 512)
        B_psB1 = sched.buf("psB1", "psB", 512, 1024)
        B_psC0 = sched.buf("psC0", "psC", 0, 512)
        B_psC1 = sched.buf("psC1", "psC", 512, 1024)
        B_psD = sched.buf("psD", "psD")
        B_psT = sched.buf("psT", "psT")
        ps_halves = [(psA[:, 0:512], B_psA0), (psA[:, 512:1024], B_psA1),
                     (psB[:, 0:512], B_psB0), (psB[:, 512:1024], B_psB1)]
        ps4 = [(psD, B_psD)] + ps_halves[:3]

        sched.op("q", lambda q: q.dma_start(out=ident, in_=ident_h.ap()), writes=[B_const])
        sched.op("q", lambda q: q.dma_start(out=bs_sb, in_=bs_h.ap()), writes=[B_const])
        sched.op("q", lambda q: q.dma_start(out=lng, in_=lng_h.ap()), writes=[B_const])
        sched.op("q", lambda q: q.dma_start(out=lnb, in_=lnb_h.ap()), writes=[B_const])
        sched.op("g", lambda g: g.dma_start(out=w012, in_=w012_h.ap()), writes=[B_const])
        sched.op("g", lambda g: g.dma_start(out=ws_sb, in_=ws_h.ap()), writes=[B_const])
        sched.op("g", lambda g: g.dma_start(out=gw_sb, in_=gw_h.ap()), writes=[B_const])
        sched.op("v", lambda v: v.memset(ones_col, 1.0), writes=[B_const])
        sched.op("v", lambda v: v.memset(ones_row, 1.0), writes=[B_const])
        sched.op("v", lambda v: v.memset(ones_blk, 1.0), writes=[B_const])
        c_eps5, c_eps6, c_neg1, c_negh, c_half, c_i2048 = (csb[:, i:i + 1] for i in range(6))
        for _ap, _val in ((c_eps5, 1e-5), (c_eps6, 1e-6), (c_neg1, -1.0),
                          (c_negh, -0.5), (c_half, 0.5), (c_i2048, 1.0 / 2048.0)):
            sched.op("v", lambda v, _ap=_ap, _val=_val: v.memset(_ap, _val),
                     writes=[B_const])

        def ln_tile(src_slice, src_buf, st_i, sq_i, dest_fn, dT_writes):
            so = st_i * 12
            s_sum = stat[:, so + 0:so + 1]
            s_sq = stat[:, so + 1:so + 2]
            s_mu = stat[:, so + 2:so + 3]
            s_sq2 = stat[:, so + 3:so + 4]
            s_var = stat[:, so + 4:so + 5]
            s_r = stat[:, so + 5:so + 6]
            s_nb = stat[:, so + 6:so + 7]
            sb = B_lnst[st_i]
            sched.op("v", lambda v: v.reduce_sum(s_sum, src_slice, axis=AX.X),
                     reads=[src_buf], writes=[sb])
            sq_v = sqscr2l[sq_i]
            sq_b = B_sqscr[sq_i]
            sched.op("s", lambda s: s.activation(sq_v, src_slice, AF.Square),
                     reads=[src_buf], writes=[sq_b])
            sched.op("v", lambda v: v.reduce_sum(s_sq, sq_v, axis=AX.X),
                     reads=[sq_b], writes=[sb])
            sched.op("v", lambda v: v.tensor_scalar_mul(s_mu, s_sum, 1.0 / 2048.0),
                     reads=[sb], writes=[sb])
            sched.op("v", lambda v: v.tensor_scalar_mul(s_sq2, s_sq, 1.0 / 2048.0),
                     reads=[sb], writes=[sb])
            sched.op("v", lambda v: v.tensor_mul(s_var, s_mu, s_mu),
                     reads=[sb], writes=[sb])
            sched.op("v", lambda v: v.tensor_sub(s_var, s_sq2, s_var),
                     reads=[sb], writes=[sb])
            sba = B_lnact[st_i]
            sched.op("s", lambda s: s.activation(s_var, s_var, AF.Ln, bias=c_eps5),
                     reads=[sb, B_const], writes=[sba])
            sched.op("s", lambda s: s.activation(s_r, s_var, AF.Exp, scale=c_negh),
                     reads=[sba, B_const], writes=[sba])
            sched.op("v", lambda v: v.scalar_tensor_tensor(
                s_nb, s_mu, -1.0, s_r, ALU.mult, ALU.mult),
                reads=[sb, sba], writes=[sb])
            xst = xnstage[sq_i]
            xstb = B_xnst[sq_i]
            sched.op("v", lambda v: v.tensor_scalar(xst, src_slice, s_r, s_nb, ALU.mult, ALU.add),
                     reads=[src_buf, sb, sba], writes=[xstb])
            for grp2 in range(2):
                def tr_fn(t, grp2=grp2, xst=xst):
                    ins = None
                    for k in range(8):
                        db = grp2 * 8 + k
                        ins = t.transpose(psT[:, k * 128:(k + 1) * 128],
                                          xst[:, db * 128:(db + 1) * 128], ident)
                    return ins
                sched.op("t", tr_fn, reads=[xstb, B_const], writes=[B_psT])
                dst = dest_fn(grp2)
                sched.op("v", lambda v, dst=dst: v.tensor_copy(dst, psT),
                         reads=[B_psT], writes=dT_writes)

        for pi in range(4):
            s_idx, H = divmod(pi, 2)
            xbase = s_idx * D * D

            lg_all = gst[:, 0:64]
            eb_all = gst[:, 64:128]
            mx_all = stat[:, 0:8]
            sm_all = stat[:, 8:16]
            rc_all = stat[:, 16:24]
            mm_all = stat[:, 24:32]

            def emit_acc_tile(grp, ti):
                accs = acc[:, grp * D:(grp + 1) * D]
                off = xbase + (1024 * H + 2 * grp) * D
                xap = bass.AP(x_h, off, [[D, 2], [16 * D, 64], [1, D]])
                sched.op("g", lambda g, accs=accs, xap=xap: g.dma_start(out=accs, in_=xap),
                         writes=[B_acc[grp]])

                def dest_fn(grp2, grp=grp):
                    return bass.AP(xnT.tensor, grp2 * 8 * 1024 + grp * 128,
                                   [[DB * 1024, P], [1024, 8], [1, P]])
                ln_tile(accs, B_acc[grp], ti % 4, ti % 2, dest_fn, [B_xnT, B_xnt[grp]])

            def emit_xg_tile(grp, ti):
                stg = stage_f32[grp % 2]
                off = xbase + (128 * 2 * grp + 64 * H) * D
                xap = bass.AP(x_h, off, [[128 * D, 2], [D, 64], [1, D]])
                sched.op("q", lambda q, stg=stg, xap=xap: q.dma_start(out=stg, in_=xap),
                         writes=[B_stage[grp % 2]])

                def dest_fn(grp2, grp=grp):
                    return bass.AP(xgom.tensor, grp2 * 8 * 1024 + grp * 128,
                                   [[DB * 1024, P], [1024, 8], [1, P]])
                ln_tile(stg, B_stage[grp % 2], ti % 4, ti % 2, dest_fn, [B_xg, B_xgt[grp]])

            def emit_S(tt):
                def s_chain(t, tt=tt):
                    ins = None
                    for db in range(DB):
                        ins = t.matmul(psD[0:NH, :],
                                       ws_sb[:, db * NH:(db + 1) * NH],
                                       xnT[:, db * 1024 + tt * 512: db * 1024 + tt * 512 + 512],
                                       start=(db == 0), stop=(db == DB - 1))
                    return ins
                sched.op("t", s_chain,
                         reads=[B_xnt[4 * tt + i] for i in range(4)] + [B_const],
                         writes=[B_psD])
                dst = sT[:, tt * 512:(tt + 1) * 512]
                sched.op("s", lambda s, dst=dst: s.activation(dst, psD[0:NH, :], AF.Silu, bias=bs_sb),
                         reads=[B_psD, B_const], writes=[B_sT])

            def emit_G_chain(ab):
                def g_chain(t, ab=ab):
                    ins = None
                    for db in range(DB):
                        ins = t.matmul(psD[:, 0:EXP],
                                       xnT[:, db * 1024 + ab * 128: db * 1024 + (ab + 1) * 128],
                                       gw_sb[:, db * EXP:(db + 1) * EXP],
                                       start=(db == 0), stop=(db == DB - 1))
                    return ins
                sched.op("t", g_chain, reads=[B_xnt[ab], B_const], writes=[B_psD])
                dstl = lg_all[:, ab * 8:(ab + 1) * 8]
                sched.op("v", lambda v, dstl=dstl: v.tensor_copy(dstl, psD[:, 0:EXP]),
                         reads=[B_psD], writes=[B_gst])

            ti = 0
            for grp in range(8):
                emit_acc_tile(grp, ti); ti += 1
                emit_xg_tile(grp, ti); ti += 1
                if grp == 3:
                    emit_S(0)
                    for ab in range(4):
                        emit_G_chain(ab)
                if grp == 7:
                    emit_S(1)
                    for ab in range(4, 8):
                        emit_G_chain(ab)

            for hh in range(NH):
                ub = hh % 3
                sched.op("g", lambda g, ub=ub, hh=hh: g.dma_start(out=upb[ub], in_=wq_h.ap()[hh]),
                         writes=[B_up[ub]])
                for tt in range(2):
                    pst, psb_ = ps_halves[(hh * 2 + tt) % 4]

                    def q_chain(t, ub=ub, tt=tt, pst=pst):
                        ins = None
                        for db in range(DB):
                            ins = t.matmul(pst, upb[ub][:, db * 128:(db + 1) * 128],
                                           xgom[:, db * 1024 + tt * 512: db * 1024 + tt * 512 + 512],
                                           start=(db == 0), stop=(db == DB - 1))
                        return ins
                    sched.op("t", q_chain,
                             reads=[B_up[ub]] + [B_xgt[4 * tt + i] for i in range(4)],
                             writes=[psb_], war_reads=[B_xg])
                    dst = bass.AP(gq.tensor, 8 * tt * 1024 + hh * 64,
                                  [[2 * HBE * 1024, P], [1024, 8], [1, 64]])
                    qn_w = [B_qn[qb] for qb in range(8 * tt, 8 * tt + 8)]
                    sched.op("s", lambda s, dst=dst, pst=pst: s.activation(dst, pst, AF.Silu),
                             reads=[psb_], writes=qn_w)

            for ab in range(8):
                sched.op("v", lambda v, ab=ab: v.reduce_max(
                    mx_all[:, ab:ab + 1], lg_all[:, ab * 8:(ab + 1) * 8], axis=AX.X),
                    reads=[B_gst], writes=[B_stat[0]])
            for ab in range(8):
                sched.op("v", lambda v, ab=ab: v.tensor_scalar_sub(
                    eb_all[:, ab * 8:(ab + 1) * 8], lg_all[:, ab * 8:(ab + 1) * 8],
                    mx_all[:, ab:ab + 1]),
                    reads=[B_gst, B_stat[0]], writes=[B_gst])
            sched.op("s", lambda s: s.activation(lg_all, eb_all, AF.Exp),
                     reads=[B_gst], writes=[B_gst])
            for ab in range(8):
                sched.op("v", lambda v, ab=ab: v.reduce_sum(
                    sm_all[:, ab:ab + 1], lg_all[:, ab * 8:(ab + 1) * 8], axis=AX.X),
                    reads=[B_gst], writes=[B_stat[0]])
            sched.op("v", lambda v: v.reciprocal(rc_all, sm_all),
                     reads=[B_stat[0]], writes=[B_stat[0]])
            for ab in range(8):
                pp = comb[:, ab * EXP:(ab + 1) * EXP]
                sched.op("v", lambda v, pp=pp, ab=ab: v.tensor_scalar_mul(
                    pp, lg_all[:, ab * 8:(ab + 1) * 8], rc_all[:, ab:ab + 1]),
                    reads=[B_gst, B_stat[0]], writes=[B_comb])
            for ab in range(8):
                pp = comb[:, ab * EXP:(ab + 1) * EXP]
                sched.op("v", lambda v, pp=pp, ab=ab: v.reduce_max(
                    mm_all[:, ab:ab + 1], pp, axis=AX.X),
                    reads=[B_comb], writes=[B_stat[0]])
            for ab in range(8):
                pp = comb[:, ab * EXP:(ab + 1) * EXP]
                ee = eb_all[:, ab * 8:(ab + 1) * 8]
                sched.op("v", lambda v, pp=pp, ee=ee, ab=ab: v.tensor_scalar(
                    ee, pp, mm_all[:, ab:ab + 1], None, ALU.is_ge),
                    reads=[B_comb, B_stat[0]], writes=[B_gst])
            for ab in range(8):
                pp = comb[:, ab * EXP:(ab + 1) * EXP]
                ee = eb_all[:, ab * 8:(ab + 1) * 8]
                sched.op("v", lambda v, pp=pp, ee=ee: v.scalar_tensor_tensor(
                    ee, ee, -1e30, pp, ALU.mult, ALU.add),
                    reads=[B_comb, B_gst], writes=[B_gst])
            for ab in range(8):
                ee = eb_all[:, ab * 8:(ab + 1) * 8]
                sched.op("v", lambda v, ee=ee, ab=ab: v.reduce_max(
                    mm_all[:, ab:ab + 1], ee, axis=AX.X),
                    reads=[B_gst], writes=[B_stat[0]])
            for ab in range(8):
                pp = comb[:, ab * EXP:(ab + 1) * EXP]
                ee = eb_all[:, ab * 8:(ab + 1) * 8]
                sched.op("v", lambda v, pp=pp, ee=ee, ab=ab: v.tensor_scalar(
                    ee, pp, mm_all[:, ab:ab + 1], None, ALU.is_ge),
                    reads=[B_comb, B_stat[0]], writes=[B_gst])
            for ab in range(8):
                pp = comb[:, ab * EXP:(ab + 1) * EXP]
                ee = eb_all[:, ab * 8:(ab + 1) * 8]
                sched.op("v", lambda v, pp=pp, ee=ee: v.tensor_mul(pp, pp, ee),
                         reads=[B_gst, B_comb], writes=[B_comb])

            if "dump" in _DBG and pi == _DPI:
                sched.op("q", lambda q: q.dma_start(out=dxnT_h.ap(), in_=xnT),
                         reads=[B_xnT])
                sched.op("q", lambda q: q.dma_start(out=dxgT_h.ap(), in_=xgom[:, 0:16384]),
                         reads=[B_xg])
                sched.op("q", lambda q: q.dma_start(out=dcomb_h.ap(), in_=comb),
                         reads=[B_comb])
                sched.op("q", lambda q: q.dma_start(out=dsT_h.ap(), in_=sT),
                         reads=[B_sT])
                sched.op("q", lambda q: q.dma_start(out=dqp_h.ap(), in_=qn_all),
                         reads=B_qn)

            for q in range(NH):
                qs = qn_all[:, q * 1024:(q + 1) * 1024]
                q2 = xgom[:, q * 1024:(q + 1) * 1024]
                sched.op("v", lambda v, q2=q2, qs=qs: v.tensor_mul(q2, qs, qs),
                         reads=[B_qn[q]], writes=[B_om[q]])

            def bf_slot(q):
                t, idx = q // 6, q % 6
                r = 32 * (idx % 3)
                c = (idx // 3) * 1024
                return upb[2 - t][r:r + 1, c:c + 1024], r

            def a2a_norm(q):
                k, r = q % 6, 32 * (q % 3)
                X = psA if (q // 3) % 2 == 0 else psB
                q2 = xgom[:, q * 1024:(q + 1) * 1024]
                row = X[r:r + 1, 0:1024]

                def ss_mm(t, X=X, r=r, q2=q2):
                    t.matmul(X[r:r + 1, 0:512], ones_col, q2[:, 0:512],
                             start=True, stop=True)
                    return t.matmul(X[r:r + 1, 512:1024], ones_col, q2[:, 512:1024],
                                    start=True, stop=True)
                sched.op("t", ss_mm, reads=[B_om[q], B_const], writes=[B_sl[k]])
                sched.op("s", lambda s, row=row: s.activation(row, row, AF.Ln),
                         reads=[B_sl[k]], writes=[B_sl[k]])
                sched.op("s", lambda s, row=row, r=r: s.activation(
                    row, row, AF.Exp, scale=c_half[r:r + 1, :]),
                    reads=[B_sl[k], B_const], writes=[B_sl[k]])
                sched.op("s", lambda s, row=row, r=r: s.activation(
                    row, row, AF.Ln, bias=c_eps5[r:r + 1, :]),
                    reads=[B_sl[k], B_const], writes=[B_sl[k]])
                bf, _r = bf_slot(q)
                sched.op("s", lambda s, bf=bf, row=row, r=r: s.activation(
                    bf, row, AF.Exp, scale=c_neg1[r:r + 1, :]),
                    reads=[B_sl[k], B_const], writes=[B_bfq[q]])

            def a2a_bcast(q):
                qs = qn_all[:, q * 1024:(q + 1) * 1024]
                bf, r = bf_slot(q)

                def rb_mm(t, bf=bf, r=r):
                    t.matmul(psC[:, 0:512], ones_blk[r:r + 1, :], bf[:, 0:512],
                             start=True, stop=True)
                    return t.matmul(psC[:, 512:1024], ones_blk[r:r + 1, :],
                                    bf[:, 512:1024], start=True, stop=True)
                sched.op("t", rb_mm, reads=[B_bfq[q], B_const],
                         writes=[B_psC0, B_psC1])
                sched.op("v", lambda v, qs=qs: v.tensor_mul(qs, qs, psC),
                         reads=[B_psC0, B_psC1, B_qn[q]], writes=[B_qn[q]])

            for q in range(NH):
                a2a_norm(q)
                if q >= 3:
                    a2a_bcast(q - 3)
            for q in range(NH - 3, NH):
                a2a_bcast(q)

            B_psCq = [sched.buf(f"psCq{i}", "psC", i * 128, (i + 1) * 128)
                      for i in range(8)]
            B_psBq = [sched.buf(f"psBq{i}", "psB", i * 128, (i + 1) * 128)
                      for i in range(8)]

            def o_ps(qb):
                if qb % 2 == 0:
                    return psC, B_psCq, B_psC0, B_psC1
                return psB, B_psBq, B_psB0, B_psB1
            B_pstq = [sched.buf(f"pstq{i}", "psT", i * 128, (i + 1) * 128)
                      for i in range(8)]
            B_z = sched.buf("zbuf", "upb1", 0, 1024)
            B_y = sched.buf("ybuf", "upb1", 1024, 2048)
            B_stM = sched.buf("stM", "stat", 0, 8, small=True)
            B_stV = sched.buf("stV", "stat", 8, 16, small=True)
            B_stQ = sched.buf("stQ", "stat", 16, 24, small=True)
            B_o2 = [sched.buf("o2a", "gq", 16384, 16640, small=True),
                    sched.buf("o2b", "gq", 16640, 16896, small=True)]
            o2sc = [gq_f32[:, 8192:8320], gq_f32[:, 8320:8448]]
            B_oc = [sched.buf("oca", "gq", 16896, 17152, small=True),
                    sched.buf("ocb", "gq", 17152, 17408, small=True)]
            ocsc = [gq_f32[:, 8448:8576], gq_f32[:, 8576:8704]]
            zt = upb[1][:, 0:1024]
            yt = upb[1][:, 1024:2048]
            m8, v8, q8 = stat[:, 0:8], stat[:, 8:16], stat[:, 16:24]
            srows = [srow_bf, gq[0:1, 20480:21504]]
            B_srows = [B_rowS, B_rowB]

            def a2_tail(qb):
                oms = xgom[:, qb * 1024:(qb + 1) * 1024]
                for i in range(8):
                    def tr_mm(t, i=i):
                        return t.transpose(psT[:, i * 128:(i + 1) * 128],
                                           zt[:, i * 128:(i + 1) * 128], ident)
                    sched.op("t", tr_mm, reads=[B_z, B_const],
                             writes=[B_pstq[i], B_psT])
                for i in range(8):
                    sched.op("v", lambda v, i=i: v.tensor_scalar(
                        yt[:, i * 128:(i + 1) * 128], psT[:, i * 128:(i + 1) * 128],
                        lng, lnb, ALU.mult, ALU.add),
                        reads=[B_pstq[i], B_const], writes=[B_y])

                def sb0_mm(t, qb=qb):
                    return t.matmul(psD, ones_row, srows[qb % 2][:, 0:512],
                                    start=True, stop=True)
                sched.op("t", sb0_mm, reads=[B_srows[qb % 2], B_const], writes=[B_psD])
                sched.op("v", lambda v, oms=oms: v.tensor_mul(
                    oms[:, 0:512], psD, yt[:, 0:512]),
                    reads=[B_psD, B_y], writes=[B_om[qb]])

                X, Bq4, _b0, Xb1 = o_ps(qb)

                def sb1_mm(t, X=X, qb=qb):
                    return t.matmul(X[:, 512:1024], ones_row,
                                    srows[qb % 2][:, 512:1024], start=True, stop=True)
                sched.op("t", sb1_mm, reads=[B_srows[qb % 2], B_const],
                         writes=[Bq4[4], Bq4[5], Bq4[6], Bq4[7], Xb1])
                sched.op("v", lambda v, oms=oms, X=X: v.tensor_mul(
                    oms[:, 512:1024], X[:, 512:1024], yt[:, 512:1024]),
                    reads=[Bq4[4], Bq4[5], Bq4[6], Bq4[7], B_y],
                    writes=[B_om[qb]])

            for qb in range(NH):
                qs = qn_all[:, qb * 1024:(qb + 1) * 1024]
                X, BXq, Xb0, Xb1 = o_ps(qb)

                def sx_mm(t, qb=qb):
                    sel = ident[0:NH, qb:qb + 1]
                    t.matmul(psA[0:1, 0:512], sel, sT[:, 0:512], start=True, stop=True)
                    return t.matmul(psA[0:1, 512:1024], sel, sT[:, 512:1024],
                                    start=True, stop=True)
                sched.op("t", sx_mm, reads=[B_sT, B_const], writes=[B_psA0, B_psA1])
                sched.op("s", lambda s, qb=qb: s.activation(
                    srows[qb % 2], psA[0:1, 0:1024], AF.Copy),
                    reads=[B_psA0, B_psA1], writes=[B_srows[qb % 2]])

                def ab_half(h, qb=qb, qs=qs):
                    def mm(t, h=h, qb=qb, qs=qs):
                        w0l = w012[:, qb * 128:(qb + 1) * 128]
                        w2l = w012[:, 2 * 2048 + qb * 128: 2 * 2048 + (qb + 1) * 128]
                        t.matmul(psA[:, 0:512], w0l, qs[:, h * 512:(h + 1) * 512],
                                 start=True, stop=True)
                        return t.matmul(psA[:, 512:1024], w2l,
                                        qs[:, h * 512:(h + 1) * 512],
                                        start=True, stop=True)
                    sched.op("t", mm, reads=[B_qn[qb], B_const],
                             writes=[B_psA0, B_psA1])
                    sched.op("s", lambda s, h=h: s.activation(
                        qsq[:, h * 512:(h + 1) * 512], psA[:, 0:512], AF.Silu),
                        reads=[B_psA0], writes=[B_scr0])
                    sched.op("v", lambda v, h=h: v.tensor_mul(
                        hid[:, h * 512:(h + 1) * 512], psA[:, 512:1024],
                        qsq[:, h * 512:(h + 1) * 512]),
                        reads=[B_psA1, B_scr0], writes=[B_scr0])

                ab_half(0)
                if qb > 0:
                    a2_tail(qb - 1)
                ab_half(1)

                for i in range(8):
                    def o_mm(t, qb=qb, i=i, X=X):
                        w1l = w012[:, 2048 + qb * 128: 2048 + (qb + 1) * 128]
                        return t.matmul(X[:, i * 128:(i + 1) * 128],
                                        hid[:, i * 128:(i + 1) * 128], w1l,
                                        start=True, stop=True)
                    sched.op("t", o_mm, reads=[B_scr0, B_const],
                             writes=[BXq[i], Xb0 if i < 4 else Xb1])
                for i in range(8):
                    pq = X[:, i * 128:(i + 1) * 128]
                    sched.op("s", lambda s, pq=pq, i=i: s.activation(
                        ocsc[i % 2], pq, AF.Copy),
                        reads=[BXq[i]], writes=[B_oc[i % 2]])
                    sched.op("v", lambda v, i=i: v.reduce_sum(
                        m8[:, i:i + 1], ocsc[i % 2], axis=AX.X),
                        reads=[B_oc[i % 2]], writes=[B_stM])
                    sched.op("s", lambda s, pq=pq, i=i: s.activation(
                        o2sc[i % 2], pq, AF.Square),
                        reads=[BXq[i]], writes=[B_o2[i % 2]])
                    sched.op("v", lambda v, i=i: v.reduce_sum(
                        v8[:, i:i + 1], o2sc[i % 2], axis=AX.X),
                        reads=[B_o2[i % 2]], writes=[B_stV])
                sched.op("v", lambda v: v.tensor_scalar_mul(m8, m8, 1.0 / 128.0),
                         reads=[B_stM], writes=[B_stM])
                sched.op("v", lambda v: v.tensor_mul(q8, m8, m8),
                         reads=[B_stM], writes=[B_stQ])
                sched.op("v", lambda v: v.scalar_tensor_tensor(
                    v8, v8, 1.0 / 128.0, q8, ALU.mult, ALU.subtract),
                    reads=[B_stV, B_stQ], writes=[B_stV])
                sched.op("s", lambda s: s.activation(v8, v8, AF.Ln, bias=c_eps6),
                         reads=[B_stV, B_const], writes=[B_stV])
                sched.op("s", lambda s: s.activation(v8, v8, AF.Exp, scale=c_negh),
                         reads=[B_stV, B_const], writes=[B_stV])
                for i in range(8):
                    pq = X[:, i * 128:(i + 1) * 128]
                    sched.op("v", lambda v, pq=pq, i=i: v.tensor_scalar(
                        zt[:, i * 128:(i + 1) * 128], pq, m8[:, i:i + 1],
                        v8[:, i:i + 1], ALU.subtract, ALU.mult),
                        reads=[BXq[i], B_stM, B_stV], writes=[B_z])
            a2_tail(NH - 1)

            if "dump" in _DBG and pi == _DPI:
                sched.op("q", lambda q: q.dma_start(out=dom_h.ap(), in_=xgom),
                         reads=B_om)

            for dq in range(4):
                wb = dq % 2
                sched.op("g", lambda g, wb=wb, dq=dq: g.dma_start(out=wo_chunk[wb], in_=wo_h.ap()[dq]),
                         writes=[B_wo[wb]])
                for ab in range(8):
                    pst, psb_ = ps4[(dq * 8 + ab) % 4]

                    def wo_chain(t, wb=wb, ab=ab, pst=pst):
                        ins = None
                        for ffb in range(DB):
                            ins = t.matmul(pst,
                                           xgom[:, ffb * 1024 + ab * 128: ffb * 1024 + (ab + 1) * 128],
                                           wo_chunk[wb][:, ffb * 512:(ffb + 1) * 512],
                                           start=(ffb == 0), stop=(ffb == DB - 1))
                        return ins
                    sched.op("t", wo_chain, reads=[B_wo[wb]] + B_om, writes=[psb_])
                    accs = acc[:, ab * D + dq * 512: ab * D + (dq + 1) * 512]
                    if "noattn" not in _DBG:
                        sched.op("v", lambda v, accs=accs, pst=pst: v.tensor_add(accs, pst, accs),
                                 reads=[psb_, B_acc[ab]], writes=[B_acc[ab]])

            def up_phase(wsrc_g, wsrc_u, hbn, gbs, gbuf):
                for hb in range(hbn):
                    bg = (2 * hb) % 3
                    bu = (2 * hb + 1) % 3
                    sched.op("g", lambda g, bg=bg, hb=hb, wsrc_g=wsrc_g:
                             g.dma_start(out=upb[bg], in_=wsrc_g[hb]),
                             writes=[B_up[bg]])
                    sched.op("g", lambda g, bu=bu, hb=hb, wsrc_u=wsrc_u:
                             g.dma_start(out=upb[bu], in_=wsrc_u[hb]),
                             writes=[B_up[bu]])
                    for tt in range(2):
                        pg = psA[:, tt * 512:(tt + 1) * 512]
                        pgb = [B_psA0, B_psA1][tt]

                        def gate_chain(t, bg=bg, tt=tt, pg=pg):
                            ins = None
                            for db in range(DB):
                                ins = t.matmul(pg, upb[bg][:, db * 128:(db + 1) * 128],
                                               xnT[:, db * 1024 + tt * 512: db * 1024 + tt * 512 + 512],
                                               start=(db == 0), stop=(db == DB - 1))
                            return ins
                        sched.op("t", gate_chain,
                                 reads=[B_up[bg]] + [B_xnt[4 * tt + i] for i in range(4)],
                                 writes=[pgb])
                        sv = sil_f32[tt]
                        svb = B_sil[tt]
                        sched.op("s", lambda s, sv=sv, pg=pg: s.activation(sv, pg, AF.Silu),
                                 reads=[pgb], writes=[svb])
                    for tt in range(2):
                        pu = psB[:, tt * 512:(tt + 1) * 512]
                        pub = [B_psB0, B_psB1][tt]

                        def up_chain(t, bu=bu, tt=tt, pu=pu):
                            ins = None
                            for db in range(DB):
                                ins = t.matmul(pu, upb[bu][:, db * 128:(db + 1) * 128],
                                               xnT[:, db * 1024 + tt * 512: db * 1024 + tt * 512 + 512],
                                               start=(db == 0), stop=(db == DB - 1))
                            return ins
                        sched.op("t", up_chain,
                                 reads=[B_up[bu]] + [B_xnt[4 * tt + i] for i in range(4)],
                                 writes=[pub])
                        dstg = gbuf[:, hb * 1024 + tt * 512: hb * 1024 + tt * 512 + 512]
                        sv = sil_f32[tt]
                        svb = B_sil[tt]
                        sched.op("v", lambda v, dstg=dstg, pu=pu, sv=sv: v.tensor_mul(dstg, pu, sv),
                                 reads=[pub, svb], writes=gbs)

            def down_phase(gbs, gbuf, wsrc_list, nhb, colw, ev_comb):
                for ci, wap in enumerate(wsrc_list):
                    cb = ci % 2
                    sched.op("g", lambda g, cb=cb, wap=wap, nhb=nhb, colw=colw: g.dma_start(
                        out=wd_chunk[cb][:, 0:nhb * colw], in_=wap),
                        writes=[B_dch[cb]])
                    for ab in range(8):
                        pst, psb_ = ps4[(ci * 8 + ab) % 4]
                        pslc = pst[:, 0:colw]

                        def d_chain(t, cb=cb, ab=ab, pslc=pslc, nhb=nhb, colw=colw):
                            ins = None
                            for hb in range(nhb):
                                ins = t.matmul(pslc,
                                               gbuf[:, hb * 1024 + ab * 128: hb * 1024 + (ab + 1) * 128],
                                               wd_chunk[cb][:, hb * colw:(hb + 1) * colw],
                                               start=(hb == 0), stop=(hb == nhb - 1))
                            return ins
                        sched.op("t", d_chain, reads=[B_dch[cb]] + gbs, writes=[psb_])
                        accs = acc[:, ab * D + ci * colw: ab * D + ci * colw + colw]
                        if ev_comb is not None:
                            if "nomoe" in _DBG:
                                continue
                            cc = comb[:, ab * EXP + ev_comb: ab * EXP + ev_comb + 1]
                            sched.op("v", lambda v, accs=accs, pslc=pslc, cc=cc:
                                     v.scalar_tensor_tensor(accs, pslc, cc, accs, ALU.mult, ALU.add),
                                     reads=[psb_, B_comb, B_acc[ab]], writes=[B_acc[ab]])
                        else:
                            if "nosh" in _DBG:
                                continue
                            sched.op("v", lambda v, accs=accs, pslc=pslc:
                                     v.tensor_add(accs, pslc, accs),
                                     reads=[psb_, B_acc[ab]], writes=[B_acc[ab]])

            for ei in range(EXP):
                gbs = [B_g[ei % 2]]
                gbuf = g_buf[ei % 2]
                up_phase(wg_h.ap()[ei], wu_h.ap()[ei], HBE, gbs, gbuf)
                if "dump" in _DBG and pi == 0 and ei == 0:
                    sched.op("q", lambda q: q.dma_start(out=dg_h.ap(), in_=g_buf[0]),
                             reads=[B_g[0]])
                down_phase(gbs, gbuf, [wd_h.ap()[ei][dq] for dq in range(4)], HBE, 512, ei)

            up_phase(swg_h.ap(), swu_h.ap(), HBS, [B_g[0], B_g[1]], gs_full)
            down_phase([B_g[0], B_g[1]], gs_full, [swd_h.ap()[pc] for pc in range(8)],
                       HBS, 256, None)

            for grp in range(8):
                accs = acc[:, grp * D:(grp + 1) * D]
                off = xbase + (1024 * H + 2 * grp) * D
                oap = bass.AP(out_h, off, [[D, 2], [16 * D, 64], [1, D]])
                sched.op("q", lambda q, accs=accs, oap=oap: q.dma_start(out=oap, in_=accs),
                         reads=[B_acc[grp]])

        for k in sched.semkeys["q"]:
            if sched.count[k]:
                sched.wait_only("q", (k, sched.count[k]))

        block = e(nc.Block())

        @block.sync
        def _(sync):
            sched.run("q", sync)

        @block.gpsimd
        def _(gp):
            sched.run("g", gp)

        @block.tensor
        def _(t):
            sched.run("t", t)

        @block.vector
        def _(v):
            sched.run("v", v)

        @block.scalar
        def _(s):
            sched.run("s", s)

    return nc


def _relayout(inputs):
    f = np.float32
    Wqkv = np.asarray(inputs["L0_attn_W_qkv"], f)
    qcols = (384 * np.arange(16)[:, None] + np.arange(128)[None, :]).reshape(-1)
    Wq = np.ascontiguousarray(Wqkv[:, qcols])
    wq = np.ascontiguousarray(Wq.reshape(16, 128, 16, 128).transpose(2, 1, 0, 3)
                              .reshape(NH, P, DB * P))
    Wo = np.asarray(inputs["L0_attn_W_o"], f)
    wo = np.ascontiguousarray(Wo.reshape(16, 128, 4, 512).transpose(2, 1, 0, 3)
                              .reshape(4, P, DB * 512))
    Ws = np.asarray(inputs["L0_attn_W_s"], f)
    ws = np.ascontiguousarray(Ws.reshape(16, 128, 16).transpose(1, 0, 2).reshape(P, DB * NH))
    gwv = np.asarray(inputs["L0_ffn_gate_w"], f)
    gw = np.ascontiguousarray(gwv.reshape(16, 128, 8).transpose(1, 0, 2).reshape(P, DB * EXP))
    w0 = np.asarray(inputs["L0_attn_w0"], f)
    w1 = np.asarray(inputs["L0_attn_w1"], f)
    w2 = np.asarray(inputs["L0_attn_w2"], f)

    def wt(w):
        return w.transpose(2, 0, 1).reshape(P, NH * P)
    w012 = np.ascontiguousarray(np.concatenate([wt(w0), wt(w1), wt(w2)], axis=1))
    wg = np.asarray(inputs["L0_ffn_w_gate"], f)
    wgr = np.ascontiguousarray(wg.reshape(8, 16, 128, 11, 128).transpose(0, 3, 2, 1, 4)
                               .reshape(EXP, HBE, P, DB * P))
    wu = np.asarray(inputs["L0_ffn_w_up"], f)
    wur = np.ascontiguousarray(wu.reshape(8, 16, 128, 11, 128).transpose(0, 3, 2, 1, 4)
                               .reshape(EXP, HBE, P, DB * P))
    wd = np.asarray(inputs["L0_ffn_w_down"], f)
    wdr = np.ascontiguousarray(wd.reshape(8, 11, 128, 4, 512).transpose(0, 3, 2, 1, 4)
                               .reshape(EXP, 4, P, HBE * 512))
    swg = np.asarray(inputs["L0_ffn_sw_gate"], f)
    swgr = np.ascontiguousarray(swg.reshape(16, 128, 22, 128).transpose(2, 1, 0, 3)
                                .reshape(HBS, P, DB * P))
    swu = np.asarray(inputs["L0_ffn_sw_up"], f)
    swur = np.ascontiguousarray(swu.reshape(16, 128, 22, 128).transpose(2, 1, 0, 3)
                                .reshape(HBS, P, DB * P))
    swd = np.asarray(inputs["L0_ffn_sw_down"], f)
    swdr = np.ascontiguousarray(swd.reshape(22, 128, 8, 256).transpose(2, 1, 0, 3)
                                .reshape(8, P, HBS * 256))
    return dict(
        wq=wq, wo=wo, ws=ws, gw=gw,
        bs=np.ascontiguousarray(np.asarray(inputs["L0_attn_b_s"], f).reshape(NH, 1)),
        lng=np.ascontiguousarray(np.asarray(inputs["L0_attn_ln_g"], f).reshape(P, 1)),
        lnb=np.ascontiguousarray(np.asarray(inputs["L0_attn_ln_b"], f).reshape(P, 1)),
        w012=w012, wg=wgr, wu=wur, wd=wdr, swg=swgr, swu=swur, swd=swdr,
        ident=np.eye(P, dtype=BFD),
    )


def kernel(**inputs):
    weights = _relayout(inputs)
    x = np.asarray(inputs["L0_attn_x"], np.float32)
    nc = build_program()
    in_maps = []
    for c in range(N_CORES):
        m = dict(weights)
        m["x"] = np.ascontiguousarray(x[SEQ_PER_CORE * c: SEQ_PER_CORE * (c + 1)])
        in_maps.append(m)
    res = run_bass_kernel_spmd(nc, in_maps, core_ids=list(range(N_CORES)))
    return np.concatenate([np.asarray(r["out"]) for r in res.results], axis=0)
